# version 06
# baseline (speedup 1.0000x reference)
import os
import numpy as np
import concourse.bass as bass
import concourse.mybir as mybir
from concourse.bass_utils import run_bass_kernel_spmd
from contextlib import ExitStack

_SIM = os.environ.get("BASS_SIM_FP64") == "1"

F32 = mybir.dt.float32
BF16 = mybir.dt.bfloat16
AOP = mybir.AluOpType
AF = mybir.ActivationFunctionType
AX = mybir.AxisListType

B, T, C = 4, 4096, 1024
NH, D = 16, 64
C3 = 3 * C
I = 2816
TQ = 2048
NTB = 8
NIC = I // 128
QMAX = 127.0
KCDM = [8, 16, 24, 32]
STRIPES = [[0, 3, 4, 7], [1, 2, 5, 6]]

BF16_NOP = 280
K64_NOP = 130
TR_NOP = 280


class Prog:
    def __init__(self, handles, sems):
        self.h = handles
        self.sems = sems
        self.cnt = {k: 0 for k in sems}
        self.lastwait = {}

    def wait(self, eng, marks):
        h = self.h[eng]
        for (s, v) in marks:
            if v <= 0:
                continue
            key = (eng, s)
            if self.lastwait.get(key, -1) >= v:
                continue
            self.lastwait[key] = v
            if h is not None:
                h.wait_ge(self.sems[s], v)

    def op(self, eng, meth, *a, waits=(), sem=None, amt=1, nopc=0, **kw):
        self.wait(eng, waits)
        h = self.h[eng]
        if sem is None:
            sem = eng
        self.cnt[sem] += amt
        if h is not None:
            ins = getattr(h, meth)(*a, **kw)
            ins.then_inc(self.sems[sem], amt)
            if nopc and not _SIM:
                h.nop(cycle_cnt=nopc)
        return (sem, self.cnt[sem])

    def mark(self, sem):
        return (sem, self.cnt[sem])

    def barrier(self):
        marks = [self.mark(s) for s in self.cnt]
        for eng in ("te", "sc", "ve", "gp", "sy"):
            self.wait(eng, marks)


def build_program(debug=False):
    nc = bass.Bass("TRN2", target_bir_lowering=False)

    x_in = nc.dram_tensor("x", [T, C], F32, kind="ExternalInput").ap()
    xq_in = nc.dram_tensor("xq", [TQ, C], F32, kind="ExternalInput").ap()
    Wa = nc.dram_tensor("Wa", [C, C3], F32, kind="ExternalInput").ap()
    ba = nc.dram_tensor("ba", [C3], F32, kind="ExternalInput").ap()
    Wp = nc.dram_tensor("Wp", [C, C], F32, kind="ExternalInput").ap()
    bp = nc.dram_tensor("bp", [C], F32, kind="ExternalInput").ap()
    gsc = nc.dram_tensor("gsc", [1], F32, kind="ExternalInput").ap()
    clipx = nc.dram_tensor("clipx", [1], F32, kind="ExternalInput").ap()
    clipd = nc.dram_tensor("clipd", [1], F32, kind="ExternalInput").ap()
    wg = nc.dram_tensor("wg", [I, C], F32, kind="ExternalInput").ap()
    wu = nc.dram_tensor("wu", [I, C], F32, kind="ExternalInput").ap()
    wd = nc.dram_tensor("wd", [C, I], F32, kind="ExternalInput").ap()
    ident_in = nc.dram_tensor("ident", [128, 128], F32, kind="ExternalInput").ap()
    m01_in = nc.dram_tensor("m01", [4, 8, 128, 512], BF16, kind="ExternalInput").ap()

    out = nc.dram_tensor("out", [TQ, C], F32, kind="ExternalOutput").ap()

    qT_s = nc.dram_tensor("qT_s", [C, TQ], F32).ap()
    kT_s = nc.dram_tensor("kT_s", [C, T], F32).ap()
    vT_s = nc.dram_tensor("vT_s", [T, C], F32).ap()
    yT_s = nc.dram_tensor("yT_s", [C, TQ], F32).ap()
    xp_s = nc.dram_tensor("xp_s", [TQ, C], F32).ap()
    hp_s = nc.dram_tensor("hp_s", [I, TQ], F32).ap()
    km_s = nc.dram_tensor("km_s", [128, 8], F32).ap()
    wgq_s = nc.dram_tensor("wgq_s", [C, I], BF16).ap()
    wuq_s = nc.dram_tensor("wuq_s", [C, I], BF16).ap()
    wdq_s = nc.dram_tensor("wdq_s", [I, C], BF16).ap()
    cc1_in = nc.dram_tensor("cc1_in", [128], F32).ap()
    cc1_out = nc.dram_tensor("cc1_out", [128], F32, addr_space="Shared").ap()
    cc2_in = nc.dram_tensor("cc2_in", [128], F32).ap()
    cc2_out = nc.dram_tensor("cc2_out", [128], F32, addr_space="Shared").ap()

    dbg = {}
    if debug:
        for nm, shp in [("d_qT", [C, TQ]), ("d_kT", [C, T]), ("d_vT", [T, C]),
                        ("d_yT", [C, TQ]), ("d_xp", [TQ, C]), ("d_hp", [I, TQ]),
                        ("d_scal", [128, 16]), ("d_xqi", [C, TQ]),
                        ("d_e", [128, 512]), ("d_q2", [128, 512]),
                        ("d_u4", [128, 512]), ("d_nrm", [128, 8]),
                        ("d_wdq", [I, C]), ("d_hq", [128, 512])]:
            dbg[nm] = nc.dram_tensor(nm, shp, F32, kind="ExternalOutput").ap()

    with ExitStack() as ctx:
        sbt = lambda name, shape, dt=F32: ctx.enter_context(nc.sbuf_tensor(name, shape, dt))

        ident = sbt("s_ident", [128, 128])
        identb = sbt("s_identb", [128, 128], BF16)
        ones = sbt("s_ones", [128, 130])
        ba_t = sbt("s_ba", [128, 24])
        wsg_t = sbt("s_wsg", [128, NIC])
        wsu_t = sbt("s_wsu", [128, NIC])
        wsd_t = sbt("s_wsd", [128, 8])
        wdsb = sbt("s_wdsb", [128, C])
        kmean = sbt("s_kmean", [128, 8])
        ksump = sbt("s_ksump", [128, 64])
        smal = sbt("s_smal", [128, 48])
        invg = sbt("s_invg", [128, 1])
        habs = sbt("s_habs", [128, 1])
        sxv = sbt("s_sxv", [128, 4])
        xqT = sbt("s_xqT", [128, 8 * TQ], BF16)
        mtiles = sbt("s_mt", [128, 8 * 512], BF16)
        pool = sbt("s_pool", [128, 24576], F32)
        s_wstage = sbt("s_wstage", [128, 512], BF16)
        s_wgu = sbt("s_wgu", [128, 4096], BF16)
        s_wd2 = sbt("s_wd2", [128, 2048], BF16)
        s_hq = sbt("s_hq", [128, 1024], BF16)
        psb = [ctx.enter_context(nc.psum_tensor(f"pb{i}", [128, 512], F32)) for i in range(8)]

        sem_names = ["te", "sc", "ve", "gpc", "cc", "dout",
                     "c_ini", "c_w0", "c_w1", "c_x", "c_wa0", "c_wa1",
                     "c_vw0", "c_vw1", "c_dbg", "c_msk", "c_km", "c_k",
                     "c_q0", "c_q1", "c_v0", "c_v1", "c_wp", "c_xr",
                     "c_y0", "c_y1", "c_h0", "c_h1", "c_wd0", "c_wd1",
                     "c_gst", "c_gkq0", "c_gkq1", "c_gv0", "c_gv1",
                     "c_gkm", "c_gyt", "c_gxp0", "c_gxp1", "c_ghp0",
                     "c_ghp1", "c_gcc", "c_gdbg", "c_go0", "c_go1", "c_xr2"]
        sems = {n: ctx.enter_context(nc.semaphore("sem_" + n)) for n in sem_names}
        block = ctx.enter_context(nc.Block())

        def P_(a, n):
            assert a + n <= 24576, (a, n)
            return pool[:, a:a + n]

        def schedule(P: Prog):
            P.op("sy", "dma_start", sem="c_ini", amt=16, out=ident[:], in_=ident_in)
            P.op("sy", "dma_start", sem="c_ini", amt=16, out=ba_t[:, 0:24],
                 in_=ba.rearrange("(c p) -> p c", p=128), allow_slow_non_contiguous=True)
            P.op("sy", "dma_start", sem="c_ini", amt=16, out=smal[:1, 0:1], in_=gsc.unsqueeze(0))
            P.op("sy", "dma_start", sem="c_ini", amt=16, out=smal[:1, 1:2], in_=clipx.unsqueeze(0))
            P.op("sy", "dma_start", sem="c_ini", amt=16, out=smal[:1, 2:3], in_=clipd.unsqueeze(0))
            m_ld0 = P.mark("c_ini")
            m_ones = P.op("ve", "memset", ones[:], 1.0)
            m_idb = P.op("ve", "tensor_copy", identb[:], ident[:], waits=[m_ld0])
            P.op("ve", "memset", habs[:], 0.0)
            P.op("ve", "memset", pool[:, 0:512], 0.0)

            m_warm = P.op("te", "matmul", psb[7][:1, :], ones[:, 0:1], pool[:, 0:512],
                          waits=[m_ones, P.mark("ve")], start=True, stop=True)

            m_rg = P.op("ve", "reciprocal", smal[:1, 4:5], smal[:1, 0:1], waits=[m_ld0])
            m_bg = P.op("te", "matmul", psb[6][:, 0:1], ones[:1, 0:128], smal[:1, 4:5],
                        waits=[m_rg, m_warm], start=True, stop=True)
            m_invg = P.op("ve", "tensor_copy", invg[:], psb[6][:, 0:1], waits=[m_bg])

            def bcast_scalar(src_ap, dst_ap, waits):
                mm = P.op("te", "matmul", psb[6][:, 4:5], ones[:1, 0:128], src_ap,
                          waits=list(waits) + [P.mark("ve")], start=True, stop=True)
                return P.op("ve", "tensor_copy", dst_ap, psb[6][:, 4:5], waits=[mm])

            def ladder_w(src_ap, n, invs_ap, acc_waits, waits):
                r = P_(4096, n)
                acc = P_(6144, n)
                bt = P_(8192, n)
                m1 = P.op("ve", "tensor_scalar", r, src_ap, invs_ap, 127.49,
                          op0=AOP.mult, op1=AOP.min, waits=list(waits))
                m2 = P.op("ve", "tensor_scalar", r, r, -128.49, 129.5,
                          op0=AOP.max, op1=AOP.add, waits=[m1])
                m3 = P.op("ve", "memset", acc, -129.0, waits=list(acc_waits))
                prev = m2
                for k in range(8, -1, -1):
                    th = float(2 ** k)
                    mb = P.op("ve", "tensor_scalar", bt, r, th, -th,
                              op0=AOP.is_ge, op1=AOP.mult, waits=[prev])
                    P.op("ve", "tensor_tensor", r, r, bt, op=AOP.add, waits=[mb])
                    prev = P.op("ve", "tensor_tensor", acc, acc, bt, op=AOP.subtract)
                return prev, P_(6144, n)

            def transpose_out(ib, ncc, dst_fn, waits):
                last_ev = P.mark("ve")
                last_dma = P.mark("c_gst")
                last_t = None
                for cc4 in range(0, ncc, 4):
                    nblk = min(4, ncc - cc4)
                    mts = None
                    for jj in range(nblk):
                        cc = cc4 + jj
                        mts = P.op("te", "transpose",
                                   psb[4][:, jj * 128:(jj + 1) * 128],
                                   ib[:, cc * 128:(cc + 1) * 128], ident[:],
                                   waits=list(waits) + [last_ev])
                    stage = s_wstage
                    mev = P.op("ve", "tensor_copy", stage[:, 0:nblk * 128],
                               psb[4][:, 0:nblk * 128],
                               waits=[mts, last_dma])
                    last_ev = mev
                    last_t = mts
                    for jj in range(nblk):
                        last_dma = P.op("gp", "dma_start", sem="c_gst", amt=16,
                                        out=dst_fn(cc4 + jj),
                                        in_=stage[:, jj * 128:(jj + 1) * 128],
                                        waits=[mev])
                return last_t

            def quant_gateup(wmat, scale_tile, outT):
                nonlocal acc_free
                buf_free = [P.mark("ve"), P.mark("ve")]
                for rc in range(NIC):
                    par = rc % 2
                    wbuf = P_(par * 2048, 1024)
                    mld = P.op("sy", "dma_start", sem="c_w%d" % par, amt=16,
                               out=wbuf, in_=wmat[rc * 128:(rc + 1) * 128, :],
                               waits=[buf_free[par]])
                    mrm = P.op("ve", "tensor_reduce", smal[:, 8:9], wbuf,
                               axis=AX.X, op=AOP.max, apply_absolute_value=True,
                               waits=[mld])
                    msc = P.op("ve", "tensor_scalar", scale_tile[:, rc:rc + 1],
                               smal[:, 8:9], 1.0 / QMAX, None, op0=AOP.mult, waits=[mrm])
                    miv = P.op("ve", "reciprocal", smal[:, 9:10],
                               scale_tile[:, rc:rc + 1], waits=[msc])
                    mq, accv = ladder_w(wbuf, 1024, smal[:, 9:10], [acc_free], [miv])
                    buf_free[par] = mq
                    acc_free = transpose_out(accv, 8,
                                             lambda cc, rc=rc: outT[cc * 128:(cc + 1) * 128,
                                                                    rc * 128:(rc + 1) * 128],
                                             [mq])

            acc_free = P.mark("te")
            quant_gateup(wg, wsg_t, wgq_s)
            quant_gateup(wu, wsu_t, wuq_s)

            for rc in range(8):
                wbuf0 = P_(0, 1408)
                wbuf1 = P_(2048, 1408)
                mld0 = P.op("sy", "dma_start", sem="c_w0", amt=16, out=wbuf0,
                            in_=wd[rc * 128:(rc + 1) * 128, 0:1408],
                            waits=[P.mark("ve")])
                mld1 = P.op("sy", "dma_start", sem="c_w1", amt=16, out=wbuf1,
                            in_=wd[rc * 128:(rc + 1) * 128, 1408:2816])
                mrm0 = P.op("ve", "tensor_reduce", smal[:, 8:9], wbuf0,
                            axis=AX.X, op=AOP.max, apply_absolute_value=True, waits=[mld0])
                mrm1 = P.op("ve", "tensor_reduce", smal[:, 10:11], wbuf1,
                            axis=AX.X, op=AOP.max, apply_absolute_value=True, waits=[mld1])
                mrm = P.op("ve", "tensor_tensor", smal[:, 8:9], smal[:, 8:9],
                           smal[:, 10:11], op=AOP.max, waits=[mrm0, mrm1])
                msc = P.op("ve", "tensor_scalar", wsd_t[:, rc:rc + 1], smal[:, 8:9],
                           1.0 / QMAX, None, op0=AOP.mult, waits=[mrm])
                miv = P.op("ve", "reciprocal", smal[:, 9:10], wsd_t[:, rc:rc + 1],
                           waits=[msc])
                for half in range(2):
                    c0 = half * 1408
                    wbuf = wbuf0 if half == 0 else wbuf1
                    mq, accv = ladder_w(wbuf, 1408, smal[:, 9:10], [acc_free], [miv])
                    acc_free = transpose_out(accv, 11,
                                             lambda cc, c0=c0, rc=rc: wdq_s[
                                                 c0 + cc * 128:c0 + (cc + 1) * 128,
                                                 rc * 128:(rc + 1) * 128], [mq])

            m_wsd = P.mark("ve")
            mts = None
            for j in range(8):
                bank = psb[4] if j < 4 else psb[5]
                jj = j % 4
                mts = P.op("te", "transpose", bank[:1, jj * 128:(jj + 1) * 128],
                           wsd_t[:, j:j + 1], ident[:], waits=[m_wsd],
                           nopc=TR_NOP)
            mrow0 = P.op("ve", "tensor_copy", P_(11776, 1024)[:1, 0:512],
                         psb[4][:1, 0:512], waits=[mts])
            mrow = P.op("ve", "tensor_copy", P_(11776, 1024)[:1, 512:1024],
                        psb[5][:1, 0:512], waits=[mts])
            for hh in range(2):
                mb = P.op("te", "matmul", psb[6][:, :], ones[:1, 0:128],
                          P_(11776, 1024)[:1, hh * 512:(hh + 1) * 512],
                          waits=[mrow, P.mark("ve")], start=True, stop=True)
                P.op("ve", "tensor_copy", wdsb[:, hh * 512:(hh + 1) * 512], psb[6][:, :],
                     waits=[mb])
            P.barrier()
            if debug:
                for r in range(NIC):
                    mldw = P.op("sy", "dma_start", sem="c_dbg", amt=16,
                                out=P_(0, 512).bitcast(BF16)[:, 0:1024],
                                in_=wdq_s[r * 128:(r + 1) * 128, :],
                                waits=[P.mark("c_gdbg"), P.mark("ve")])
                    mcvw = P.op("ve", "tensor_copy", P_(1024, 1024),
                                P_(0, 512).bitcast(BF16)[:, 0:1024], waits=[mldw])
                    P.op("gp", "dma_start", sem="c_gdbg", amt=16,
                         out=dbg["d_wdq"][r * 128:(r + 1) * 128, :], in_=P_(1024, 1024),
                         waits=[mcvw])
                P.barrier()

            bvb = P_(20480, 1024)
            mbv = P.op("sy", "dma_start", sem="c_ini", amt=16, out=bvb[:1, :],
                       in_=ba[2 * C:3 * C].unsqueeze(0))
            for hh in range(2):
                mb = P.op("te", "matmul", psb[6][:, :], ones[:1, 0:128],
                          bvb[:1, hh * 512:(hh + 1) * 512],
                          waits=[mbv, P.mark("ve")], start=True, stop=True)
                P.op("ve", "tensor_copy", bvb[:, hh * 512:(hh + 1) * 512], psb[6][:, :],
                     waits=[mb])
            m_bvb = P.mark("ve")

            def ln_tile(xt_ap, waits):
                sq = P_(4096, 1024)
                mS = P.op("ve", "tensor_reduce", smal[:, 12:13], xt_ap, axis=AX.X,
                          op=AOP.add, waits=list(waits))
                mM = P.op("ve", "tensor_scalar", smal[:, 12:13], smal[:, 12:13],
                          1.0 / C, None, op0=AOP.mult, waits=[mS])
                mQ = P.op("ve", "tensor_tensor", sq, xt_ap, xt_ap, op=AOP.mult,
                          waits=list(waits))
                mS2 = P.op("ve", "tensor_reduce", smal[:, 13:14], sq, axis=AX.X,
                           op=AOP.add, waits=[mQ])
                mE2 = P.op("ve", "tensor_scalar", smal[:, 13:14], smal[:, 13:14],
                           1.0 / C, None, op0=AOP.mult, waits=[mS2])
                mMM = P.op("ve", "tensor_tensor", smal[:, 14:15], smal[:, 12:13],
                           smal[:, 12:13], op=AOP.mult, waits=[mM])
                mV = P.op("ve", "tensor_tensor", smal[:, 13:14], smal[:, 13:14],
                          smal[:, 14:15], op=AOP.subtract, waits=[mE2, mMM])
                mVe = P.op("ve", "tensor_scalar", smal[:, 13:14], smal[:, 13:14],
                           1e-5, None, op0=AOP.add, waits=[mV])
                mR = P.op("ve", "reciprocal", smal[:, 13:14], smal[:, 13:14], waits=[mVe])
                mRs = P.op("sc", "activation", out=smal[:, 15:16], in_=smal[:, 13:14],
                           func=AF.Sqrt, waits=[mR])
                return P.op("ve", "tensor_scalar", xt_ap, xt_ap, smal[:, 12:13],
                            smal[:, 15:16], op0=AOP.subtract, op1=AOP.mult,
                            waits=[mRs])

            def transpose_xh(xt_tiles_marks):
                marks = []
                last_ev = P.mark("ve")
                for fc in range(8):
                    mts = None
                    for tt in range(4):
                        mts = P.op("te", "transpose", psb[4][:, tt * 128:(tt + 1) * 128],
                                   P_(tt * 1024, 1024)[:, fc * 128:(fc + 1) * 128],
                                   ident[:],
                                   waits=[xt_tiles_marks[tt], last_ev])
                    mev = P.op("ve", "tensor_copy", P_(5120 + fc * 512, 512),
                               psb[4][:, :], waits=[mts])
                    last_ev = mev
                    marks.append(mev)
                return marks

            wa_free = [P.mark("te"), P.mark("te")]
            kq_stage_free = [P.mark("c_gkq0"), P.mark("c_gkq1")]
            v_stage_free = [P.mark("c_gv0"), P.mark("c_gv1")]
            vw_free = [P.mark("te"), P.mark("te")]
            v_ps_free = [P.mark("ve"), P.mark("ve")]
            m_abs_psA = [P.mark("sc"), P.mark("sc")]

            def qkv_pass(src_ap, tb, do_q, do_kv, tb_cols):
                mx = []
                for tt in range(4):
                    mx.append(P.op("sy", "dma_start", sem="c_x", amt=16,
                                   out=P_(tt * 1024, 1024),
                                   in_=src_ap[tb * 512 + tt * 128:tb * 512 + (tt + 1) * 128, :],
                                   waits=[P.mark("te"), P.mark("ve")]))
                mln = []
                prev = P.mark("ve")
                for tt in range(4):
                    m = ln_tile(P_(tt * 1024, 1024), [mx[3], prev])
                    mln.append(m)
                    prev = m
                mT = transpose_xh(mln)
                chunks = (list(range(8)) if do_q else []) + (list(range(8, 16)) if do_kv else [])
                for cchunk in chunks:
                    pv = cchunk % 2
                    pbk = psb[0 + pv]
                    mmm = None
                    for fc in range(8):
                        mmm = P.op("te", "matmul", pbk[:, :],
                                   P_(9216 + fc * 1024, 1024)[:, (cchunk % 8) * 128:
                                                              (cchunk % 8) * 128 + 128],
                                   P_(5120 + fc * 512, 512),
                                   waits=[m_wk, mT[fc], m_abs_psA[pv]],
                                   start=(fc == 0), stop=(fc == 7))
                    stg = P_(18432 + pv * 512, 512)
                    if cchunk >= 8:
                        kc_i = cchunk - 8
                        mev = P.op("sc", "activation", out=stg, in_=pbk[:, :],
                                   func=AF.Identity, bias=ba_t[:, cchunk:cchunk + 1],
                                   accum_out=ksump[:, kc_i * 8 + tb:kc_i * 8 + tb + 1],
                                   waits=[mmm, kq_stage_free[pv]])
                        dst = kT_s[kc_i * 128:(kc_i + 1) * 128, tb_cols]
                    else:
                        mev = P.op("sc", "activation", out=stg, in_=pbk[:, :],
                                   func=AF.Identity, bias=ba_t[:, cchunk:cchunk + 1],
                                   waits=[mmm, kq_stage_free[pv]])
                        dst = qT_s[cchunk * 128:(cchunk + 1) * 128, tb_cols]
                    m_abs_psA[pv] = mev
                    kq_stage_free[pv] = P.op("gp", "dma_start", sem="c_gkq%d" % pv,
                                             amt=16, out=dst, in_=stg, waits=[mev])
                if do_kv:
                    for tt in range(4):
                        for vh in range(2):
                            pv = (tt * 2 + vh) % 2
                            pbv = psb[2 + pv]
                            wslot = P_(17408 + pv * 512, 512)
                            mmm = None
                            for fc in range(8):
                                mws = P.op("sy", "dma_start", sem="c_vw%d" % pv, amt=16,
                                           out=wslot,
                                           in_=Wa[fc * 128:(fc + 1) * 128,
                                                  2 * C + vh * 512:2 * C + (vh + 1) * 512],
                                           waits=[vw_free[pv], P.mark("te")])
                                mmm = P.op("te", "matmul", pbv[:, :],
                                           P_(5120 + fc * 512, 512)[:, tt * 128:(tt + 1) * 128],
                                           wslot,
                                           waits=[mws, mT[fc], v_ps_free[pv]],
                                           start=(fc == 0), stop=(fc == 7))
                            vw_free[pv] = mmm
                            stg = P_(19456 + pv * 512, 512)
                            mev = P.op("ve", "tensor_tensor", stg, pbv[:, :],
                                       bvb[:, vh * 512:(vh + 1) * 512], op=AOP.add,
                                       waits=[mmm, m_bvb, v_stage_free[pv]])
                            v_ps_free[pv] = mev
                            v_stage_free[pv] = P.op(
                                "gp", "dma_start", sem="c_gv%d" % pv, amt=16,
                                out=vT_s[tb * 512 + tt * 128:tb * 512 + (tt + 1) * 128,
                                         vh * 512:(vh + 1) * 512],
                                in_=stg, waits=[mev])

            m_wk = None
            for fc in range(8):
                m_wk = P.op("sy", "dma_start", sem="c_wa0", amt=16,
                            out=P_(9216 + fc * 1024, 1024),
                            in_=Wa[fc * 128:(fc + 1) * 128, C:2 * C],
                            waits=[P.mark("ve"), P.mark("te")])
            for tb in range(NTB):
                qkv_pass(x_in, tb, False, True, slice(tb * 512, (tb + 1) * 512))
            for fc in range(8):
                m_wk = P.op("sy", "dma_start", sem="c_wa0", amt=16,
                            out=P_(9216 + fc * 1024, 1024),
                            in_=Wa[fc * 128:(fc + 1) * 128, 0:C],
                            waits=[P.mark("te")])
            for tb in range(4):
                qkv_pass(xq_in, tb, True, False, slice(tb * 512, (tb + 1) * 512))

            mkm = []
            for kc_i in range(8):
                mk1 = P.op("ve", "tensor_reduce", kmean[:, kc_i:kc_i + 1],
                           ksump[:, kc_i * 8:(kc_i + 1) * 8], axis=AX.X, op=AOP.add,
                           waits=[P.mark("sc")])
                mkm.append(P.op("ve", "tensor_scalar", kmean[:, kc_i:kc_i + 1],
                                kmean[:, kc_i:kc_i + 1], 1.0 / T, None,
                                op0=AOP.mult, waits=[mk1]))
            m_km = P.op("gp", "dma_start", sem="c_gkm", amt=16, out=km_s[:, :],
                        in_=kmean[:, 0:8], waits=mkm)

            if debug:
                P.barrier()
                mkd = P.op("ve", "tensor_copy", P_(8192, 8), kmean[:, 0:8],
                           waits=[m_km])
                P.op("gp", "dma_start", sem="c_gdbg", amt=16, out=dbg["d_scal"][:, 8:16],
                     in_=P_(8192, 8), waits=[mkd])
                for r in range(8):
                    mld = P.op("sy", "dma_start", sem="c_dbg", amt=16, out=P_(0, 2048),
                               in_=qT_s[r * 128:(r + 1) * 128, :],
                               waits=[P.mark("c_gdbg"), P.mark("ve")])
                    P.op("gp", "dma_start", sem="c_gdbg", amt=16,
                         out=dbg["d_qT"][r * 128:(r + 1) * 128, :], in_=P_(0, 2048),
                         waits=[mld])
                for r in range(8):
                    mld = P.op("sy", "dma_start", sem="c_dbg", amt=16, out=P_(0, 4096),
                               in_=kT_s[r * 128:(r + 1) * 128, :], waits=[P.mark("c_gdbg")])
                    P.op("gp", "dma_start", sem="c_gdbg", amt=16,
                         out=dbg["d_kT"][r * 128:(r + 1) * 128, :], in_=P_(0, 4096),
                         waits=[mld])
                for r in range(32):
                    mld = P.op("sy", "dma_start", sem="c_dbg", amt=16, out=P_(0, 1024),
                               in_=vT_s[r * 128:(r + 1) * 128, :], waits=[P.mark("c_gdbg")])
                    P.op("gp", "dma_start", sem="c_gdbg", amt=16,
                         out=dbg["d_vT"][r * 128:(r + 1) * 128, :], in_=P_(0, 1024),
                         waits=[mld])


            P.barrier()

            kbuf = P_(0, 4096)
            ascr = P_(17920, 512)
            madp = P_(18432, 32)
            mad4 = P_(18464, 4)
            wch1 = P_(18468, 4)
            wch2 = P_(18472, 4)
            s4p = P_(18476, 4)
            wden = P_(18480, 4)
            w4 = P_(18484, 4)
            s4row = P_(18496, 512)
            wrow = P_(19008, 512)
            yt = P_(19520, 512)
            kmcur = P_(20032, 1)
            wbt = P_(20096, 512)

            mvo = []
            for hp in range(2):
                vb = P_(5120 + hp * 4096, 4096).rearrange("p (k c) -> p k c", c=128)
                m1_ = P.op("ve", "memset", vb[:, :, 0:1], 1.0)
                mvo.append(P.op("ve", "memset", vb[:, :, 1:64], 0.0, waits=[m1_]))

            km_free = P.mark("ve")
            bband_free = [P.mark("ve"), P.mark("ve"), P.mark("ve")]
            kbuf_free = [P.mark("te"), P.mark("ve")]
            q_free = [P.mark("te"), P.mark("te")]
            v_free = [mvo[0], mvo[1]]
            u4_free = [P.mark("te"), P.mark("te"), P.mark("te")]
            mask_free = P.mark("ve")
            yt_free = P.mark("c_gyt")
            s4_free = P.mark("te")
            s5_free = P.mark("ve")
            wbt_free = P.mark("ve")

            for h in range(NH):
                hp = h % 2
                mkm_l = P.op("sy", "dma_start", sem="c_km", amt=16,
                             out=kmcur[:64, :],
                             in_=km_s[(h % 2) * 64:(h % 2) * 64 + 64,
                                      h // 2:h // 2 + 1],
                             waits=[km_free, m_km],
                             allow_slow_non_contiguous=True)
                mkl = P.op("sy", "dma_start", sem="c_k", amt=16,
                           out=kbuf[:64, :],
                           in_=kT_s[h * 64:h * 64 + 64, :],
                           waits=kbuf_free)
                mkc = P.op("ve", "tensor_scalar", kbuf[:64, :], kbuf[:64, :],
                           kmcur[:64, :], None, op0=AOP.subtract,
                           waits=[mkl, mkm_l])
                km_free = mkc
                vslot = P_(5120 + hp * 4096, 4096)
                mvl = P.op("sy", "dma_start", sem="c_v%d" % hp, amt=16,
                           out=vslot.rearrange("p (k c) -> p k c", c=128)[:, :, 64:128],
                           in_=vT_s.rearrange("(k p) c -> p k c", p=128)[:, :, h * 64:(h + 1) * 64],
                           waits=[v_free[hp]])
                for j in range(4):
                    kcd = KCDM[j]
                    mmask = P.op("sy", "dma_start", sem="c_msk", amt=16,
                                 out=mtiles.rearrange("p (a c) -> p a c", c=512),
                                 in_=m01_in[j].transpose([1, 0, 2]),
                                 waits=[mask_free])
                    qsl = P_(4096 + (j % 2) * 512, 512)
                    mql = P.op("sy", "dma_start", sem="c_q%d" % (j % 2), amt=16,
                               out=qsl[:64, :],
                               in_=qT_s[h * 64:h * 64 + 64, j * 512:(j + 1) * 512],
                               waits=[q_free[j % 2]])

                    mabs = [None, None]
                    last_abs = P.mark("sc")
                    a_steps = [(qt, kb) for qt in range(4) for kb in range(8)]
                    a_pos = 0

                    def emit_a(nsteps):
                        nonlocal a_pos, last_abs
                        for _ in range(nsteps):
                            if a_pos >= len(a_steps):
                                return
                            qt, kb = a_steps[a_pos]
                            pa = a_pos % 2
                            a_pos += 1
                            pba = psb[0 + pa]
                            w = [mql, mkc]
                            if mabs[pa] is not None:
                                w.append(mabs[pa])
                            mA = P.op("te", "matmul", pba[:, :],
                                      qsl[:64, qt * 128:(qt + 1) * 128],
                                      kbuf[:64, kb * 512:(kb + 1) * 512],
                                      waits=w, start=True, stop=True,
                                      nopc=K64_NOP)
                            mabs[pa] = P.op("sc", "activation", out=ascr, in_=pba[:, :],
                                            func=AF.Abs,
                                            accum_out=madp[:, qt * 8 + kb:qt * 8 + kb + 1],
                                            waits=[mA, last_abs])
                            last_abs = mabs[pa]

                    mq2 = [None, None, None]
                    mu4 = [None, None, None]
                    mav = None
                    bbanks = [2, 3, 7]
                    for kc in range(kcd):
                        pb_ = kc % 3
                        pbs = psb[bbanks[pb_]]
                        etile = P_(13312 + pb_ * 512, 512)
                        q2t = P_(14848 + pb_ * 512, 512)
                        u4t = P_(16384 + pb_ * 512, 512)
                        w = [mkc, mql]
                        if mq2[pb_] is not None:
                            w.append(mq2[pb_])
                        else:
                            w.append(bband_free[pb_])
                        mB = P.op("te", "matmul", pbs[:, :],
                                  kbuf[:64, kc * 128:(kc + 1) * 128],
                                  qsl[:64, :],
                                  waits=w, start=True, stop=True, nopc=K64_NOP)
                        emit_a(2 if kcd <= 16 else 1)
                        woff = kc - (kcd - 8)
                        we = [mB]
                        if mq2[pb_] is not None:
                            we.append(mq2[pb_])
                        if woff >= 0:
                            me = P.op("ve", "tensor_tensor", etile, pbs[:, :],
                                      mtiles.rearrange("p (a c) -> p a c", c=512)[:, woff, :],
                                      op=AOP.mult, waits=[mmask] + we)
                        else:
                            me = P.op("ve", "tensor_copy", etile, pbs[:, :], waits=we)
                        mq2[pb_] = P.op("ve", "scalar_tensor_tensor", q2t, pbs[:, :],
                                        0.0, etile, op0=AOP.max, op1=AOP.mult,
                                        waits=[me])
                        if debug and j == 0 and h == 0 and kc == 0:
                            P.op("gp", "dma_start", sem="c_gdbg", amt=16,
                                 out=dbg["d_e"], in_=etile, waits=[mq2[pb_]])
                            P.op("gp", "dma_start", sem="c_gdbg", amt=16,
                                 out=dbg["d_q2"], in_=q2t, waits=[mq2[pb_]])
                        mu4[pb_] = P.op("sc", "activation", out=u4t, in_=q2t,
                                        func=AF.Square,
                                        waits=[mq2[pb_], u4_free[pb_]])
                        if debug and j == 0 and h == 0 and kc == 0:
                            P.op("gp", "dma_start", sem="c_gdbg", amt=16,
                                 out=dbg["d_u4"], in_=u4t, waits=[mu4[pb_]])
                        mav = P.op("te", "matmul", psb[4][:, :],
                                   vslot.rearrange("p (k c) -> p k c", c=128)[:, kc, :],
                                   u4t,
                                   waits=[mu4[pb_], mvl, s4_free,
                                          P.mark("c_gdbg") if debug else ("te", 0)],
                                   start=(kc == 0), stop=(kc == kcd - 1))
                        u4_free[pb_] = mav
                    emit_a(len(a_steps))
                    kbuf_free = [mabs[0], mabs[1], mav, mkc]
                    q_free[j % 2] = mav
                    mask_free = P.mark("ve")
                    mmadr = []
                    for qt in range(4):
                        mmadr.append(P.op("ve", "tensor_reduce", mad4[:, qt:qt + 1],
                                          madp[:, qt * 8:(qt + 1) * 8], axis=AX.X,
                                          op=AOP.add, waits=[mabs[0], mabs[1]]))
                    mmad = P.op("ve", "tensor_scalar", mad4, mad4, 1.0 / T, 1e-6,
                                op0=AOP.mult, op1=AOP.add, waits=mmadr)
                    mw1 = P.op("ve", "tensor_scalar", wch1, mad4, invg[:, 0:1], None,
                               op0=AOP.mult, waits=[mmad, m_invg])
                    mw2 = P.op("ve", "tensor_tensor", wch2, wch1, wch1, op=AOP.mult,
                               waits=[mw1])
                    mw3 = P.op("ve", "tensor_tensor", wch1, wch2, wch2, op=AOP.mult,
                               waits=[mw2])
                    mw4 = P.op("ve", "tensor_scalar", wch1, wch1, 1e-6, None,
                               op0=AOP.mult, waits=[mw3])

                    ms4 = P.op("ve", "tensor_copy", s4row[:1, :], psb[4][0:1, :],
                               waits=[mav])
                    mt4 = None
                    for qt in range(4):
                        mt4 = P.op("te", "transpose", psb[5][:, qt:qt + 1],
                                   s4row[:1, qt * 128:(qt + 1) * 128], ident[:1, :1],
                                   waits=[ms4, s5_free])
                    ms4p = P.op("ve", "tensor_copy", s4p, psb[5][:, 0:4], waits=[mt4])
                    mden = P.op("ve", "tensor_tensor", wden, s4p, wch1, op=AOP.add,
                                waits=[ms4p, mw4])
                    mw4v = P.op("ve", "reciprocal", w4, wden, waits=[mden])
                    mtw = None
                    for qt in range(4):
                        mtw = P.op("te", "transpose", psb[5][:1, qt * 128:(qt + 1) * 128],
                                   w4[:, qt:qt + 1], ident[:],
                                   waits=[mw4v, ms4p])
                    mwr = P.op("ve", "tensor_copy", wrow[:1, :], psb[5][:1, 0:512],
                               waits=[mtw])
                    mwb = P.op("te", "matmul", psb[6][:, :], ones[:1, 0:128],
                               wrow[:1, :], waits=[mwr, wbt_free], start=True,
                               stop=True)
                    mwbe = P.op("ve", "tensor_copy", wbt, psb[6][:, :], waits=[mwb])
                    wbt_free = mwbe
                    s5_free = mwr
                    for pp in range(3):
                        if mq2[pp] is not None:
                            bband_free[pp] = mq2[pp]
                    if debug and j == 0 and h == 0:
                        mdn1 = P.op("ve", "tensor_copy", P_(19200, 4), s4p, waits=[ms4p])
                        mdn2 = P.op("ve", "tensor_copy", P_(19204, 4), mad4, waits=[mmad, mdn1])
                        mdn3 = P.op("ve", "tensor_copy", P_(19208, 4), wch1, waits=[mw4, mdn2])
                        mdn4 = P.op("ve", "tensor_copy", P_(19212, 4), w4, waits=[mw4v, mdn3])
                        P.op("gp", "dma_start", sem="c_gdbg", amt=16,
                             out=dbg["d_nrm"][:, 0:8],
                             in_=P_(19200, 8), waits=[mdn1, mdn2, mdn3, mdn4])
                    myt = P.op("ve", "tensor_tensor", yt[64:128, :], psb[4][64:128, :],
                               wbt[64:128, :], op=AOP.mult,
                               waits=[mwbe, yt_free] + ([P.mark("c_gdbg")] if debug else []))
                    s4_free = myt
                    yt_free = P.op("gp", "dma_start", sem="c_gyt", amt=16,
                                   out=yT_s[h * 64:(h + 1) * 64, j * 512:(j + 1) * 512],
                                   in_=yt[64:128, :], waits=[myt])
                v_free[hp] = mav

            if debug:
                P.barrier()
                for r in range(8):
                    mld = P.op("sy", "dma_start", sem="c_dbg", amt=16, out=P_(0, 2048),
                               in_=yT_s[r * 128:(r + 1) * 128, :], waits=[P.mark("c_gdbg")])
                    P.op("gp", "dma_start", sem="c_gdbg", amt=16,
                         out=dbg["d_yT"][r * 128:(r + 1) * 128, :], in_=P_(0, 2048),
                         waits=[mld])

            P.barrier()

            bpb = P_(14336, 1024)
            mbp = P.op("sy", "dma_start", sem="c_ini", amt=16, out=bpb[:1, :],
                       in_=bp.unsqueeze(0))
            for hh in range(2):
                mb = P.op("te", "matmul", psb[6][:, :], ones[:1, 0:128],
                          bpb[:1, hh * 512:(hh + 1) * 512],
                          waits=[mbp, P.mark("ve")], start=True, stop=True)
                P.op("ve", "tensor_copy", bpb[:, hh * 512:(hh + 1) * 512], psb[6][:, :],
                     waits=[mb])
            m_bpb = P.mark("ve")
            mwp = []
            for cc in range(8):
                mwp.append(P.op("sy", "dma_start", sem="c_wp", amt=16,
                                out=P_(cc * 1024, 1024),
                                in_=Wp[cc * 128:(cc + 1) * 128, :],
                                waits=[P.mark("te"), P.mark("ve")]))
            yst_free = [P.mark("te"), P.mark("te")]
            stage_free = [P.mark("c_gxp0"), P.mark("c_gxp1")]
            prev_evac = P.mark("ve")
            for pass4 in range(4):
                mxr = []
                for tt in range(4):
                    mxr.append(P.op("sy", "dma_start", sem="c_xr", amt=16,
                                    out=P_(9216 + tt * 1024, 1024),
                                    in_=xq_in[pass4 * 512 + tt * 128:
                                              pass4 * 512 + (tt + 1) * 128, :],
                                    waits=[prev_evac]))
                mmm = [None] * 8
                for cc in range(8):
                    pv = cc % 2
                    yslot = P_(8192 + pv * 512, 512)
                    myl = P.op("sy", "dma_start", sem="c_y%d" % pv, amt=16,
                               out=yslot,
                               in_=yT_s[cc * 128:(cc + 1) * 128,
                                        pass4 * 512:(pass4 + 1) * 512],
                               waits=[yst_free[pv]])
                    mlast = None
                    for tt in range(4):
                        for oh in range(2):
                            pbi = tt * 2 + oh
                            w = [myl, mwp[7]]
                            if mmm[pbi] is not None:
                                w.append(mmm[pbi])
                            elif pass4 > 0:
                                w.append(prev_evac)
                            mlast = P.op("te", "matmul", psb[pbi][:, :],
                                         yslot[:, tt * 128:(tt + 1) * 128],
                                         P_(cc * 1024, 1024)[:, oh * 512:(oh + 1) * 512],
                                         waits=w, start=(cc == 0), stop=(cc == 7))
                            mmm[pbi] = mlast
                    yst_free[pv] = mlast
                for tt in range(4):
                    for oh in range(2):
                        pbi = tt * 2 + oh
                        pv = pbi % 2
                        stg = P_(13312 + pv * 512, 512)
                        m1 = P.op("ve", "tensor_tensor", stg, psb[pbi][:, :],
                                  bpb[:, oh * 512:(oh + 1) * 512], op=AOP.add,
                                  waits=[mmm[pbi], m_bpb, stage_free[pv]])
                        m2 = P.op("ve", "tensor_tensor", stg, stg,
                                  P_(9216 + tt * 1024, 1024)[:, oh * 512:(oh + 1) * 512],
                                  op=AOP.add, waits=[m1, mxr[3]])
                        stage_free[pv] = P.op("gp", "dma_start", sem="c_gxp%d" % pv, amt=16,
                                              out=xp_s[pass4 * 512 + tt * 128:
                                                       pass4 * 512 + (tt + 1) * 128,
                                                       oh * 512:(oh + 1) * 512],
                                              in_=stg, waits=[m2])
                prev_evac = P.mark("ve")

            if debug:
                P.barrier()
                for r in range(16):
                    mld = P.op("sy", "dma_start", sem="c_dbg", amt=16, out=P_(0, 1024),
                               in_=xp_s[r * 128:(r + 1) * 128, :], waits=[P.mark("c_gdbg")])
                    P.op("gp", "dma_start", sem="c_gdbg", amt=16,
                         out=dbg["d_xp"][r * 128:(r + 1) * 128, :], in_=P_(0, 1024),
                         waits=[mld])

            P.barrier()

            m2c = P_(5120, 16)
            r2c = P_(5136, 16)
            rsc = P_(5152, 16)
            gacc = P_(5220, 1)
            mgacc = P.op("ve", "memset", gacc, 0.0)
            xt = P_(0, 1024)
            xt_free = P.mark("ve")
            for ti in range(16):
                mld = P.op("sy", "dma_start", sem="c_x", amt=16, out=xt,
                           in_=xp_s[ti * 128:(ti + 1) * 128, :], waits=[xt_free])
                sq = P_(1024, 1024)
                mS = P.op("ve", "tensor_reduce", m2c[:, ti:ti + 1], xt, axis=AX.X,
                          op=AOP.add, waits=[mld])
                mM = P.op("ve", "tensor_scalar", m2c[:, ti:ti + 1], m2c[:, ti:ti + 1],
                          1.0 / C, None, op0=AOP.mult, waits=[mS])
                mQ = P.op("ve", "tensor_tensor", sq, xt, xt, op=AOP.mult, waits=[mld])
                mS2 = P.op("ve", "tensor_reduce", smal[:, 12:13], sq, axis=AX.X,
                           op=AOP.add, waits=[mQ])
                mE2 = P.op("ve", "tensor_scalar", smal[:, 12:13], smal[:, 12:13],
                           1.0 / C, None, op0=AOP.mult, waits=[mS2])
                mMM = P.op("ve", "tensor_tensor", smal[:, 13:14], m2c[:, ti:ti + 1],
                           m2c[:, ti:ti + 1], op=AOP.mult, waits=[mM])
                mV = P.op("ve", "tensor_tensor", smal[:, 12:13], smal[:, 12:13],
                          smal[:, 13:14], op=AOP.subtract, waits=[mE2, mMM])
                mVe = P.op("ve", "tensor_scalar", smal[:, 12:13], smal[:, 12:13],
                           1e-5, None, op0=AOP.add, waits=[mV])
                mR = P.op("ve", "reciprocal", smal[:, 12:13], smal[:, 12:13], waits=[mVe])
                mRs = P.op("sc", "activation", out=r2c[:, ti:ti + 1], in_=smal[:, 12:13],
                           func=AF.Sqrt, waits=[mR])
                mdx = P.op("ve", "tensor_scalar", sq, xt, m2c[:, ti:ti + 1], None,
                           op0=AOP.subtract, accum_out=smal[:, 14:15], op1=AOP.max,
                           waits=[mM])
                mdn = P.op("ve", "tensor_scalar", sq, xt, m2c[:, ti:ti + 1], None,
                           op0=AOP.subtract, accum_out=smal[:, 15:16], op1=AOP.min,
                           waits=[mdx])
                mneg = P.op("ve", "tensor_scalar", smal[:, 15:16], smal[:, 15:16],
                            -1.0, None, op0=AOP.mult, waits=[mdn])
                mmx = P.op("ve", "tensor_tensor", smal[:, 14:15], smal[:, 14:15],
                           smal[:, 15:16], op=AOP.max, waits=[mneg])
                mdev = P.op("ve", "tensor_tensor", smal[:, 14:15], smal[:, 14:15],
                            r2c[:, ti:ti + 1], op=AOP.mult, waits=[mmx, mRs])
                mgacc = P.op("ve", "tensor_tensor", gacc, gacc, smal[:, 14:15],
                             op=AOP.max, waits=[mdev, mgacc])
                xt_free = mdn
            mcc_in = P.op("gp", "dma_start", sem="c_gcc", amt=16, out=cc1_in[:],
                          in_=gacc.squeeze(1), waits=[mgacc])
            mcc = P.op("gp", "collective_compute", "AllReduce", AOP.max,
                       sem="cc", amt=1, replica_groups=[list(range(8))],
                       ins=[cc1_in[:]], outs=[cc1_out[:]], waits=[mcc_in])
            mccb = P.op("gp", "dma_start", sem="c_gcc", amt=16, out=P_(5221, 1),
                        in_=cc1_out[:].unsqueeze(1), waits=[mcc])
            mgr = P.op("gp", "tensor_reduce", P_(5222, 1)[:1, :], P_(5221, 1),
                       sem="gpc", axis=AX.C, op=AOP.max, waits=[mccb])
            msig = P.op("sc", "activation", out=smal[:1, 5:6], in_=smal[:1, 1:2],
                        func=AF.Sigmoid, waits=[m_ld0])
            msx = P.op("ve", "tensor_tensor", smal[:1, 5:6], smal[:1, 5:6],
                       P_(5222, 1)[:1, :], op=AOP.mult, waits=[msig, mgr])
            msx2 = P.op("ve", "tensor_scalar", smal[:1, 5:6], smal[:1, 5:6],
                        1.0 / QMAX, None, op0=AOP.mult, waits=[msx])
            mix = P.op("ve", "reciprocal", smal[:1, 6:7], smal[:1, 5:6], waits=[msx2])
            mbx = bcast_scalar(smal[:1, 5:6], sxv[:, 1:2], [msx2])
            mbix = bcast_scalar(smal[:1, 6:7], sxv[:, 0:1], [mix])
            mrs = P.op("ve", "tensor_scalar", rsc, r2c, sxv[:, 0:1], None,
                       op0=AOP.mult, waits=[mbix])

            xt_free = P.mark("ve")
            xq_ev_free = P.mark("ve")
            acc_tr_free = P.mark("te")
            for ti in range(16):
                mld = P.op("sy", "dma_start", sem="c_x", amt=16, out=xt,
                           in_=xp_s[ti * 128:(ti + 1) * 128, :], waits=[xt_free])
                r = P_(2048, 1024)
                acc = P_(3072, 1024)
                bt = P_(4096, 1024)
                m1 = P.op("ve", "tensor_scalar", r, xt, m2c[:, ti:ti + 1],
                          rsc[:, ti:ti + 1], op0=AOP.subtract, op1=AOP.mult,
                          waits=[mld, mrs])
                m2_ = P.op("ve", "tensor_scalar", r, r, 127.49, -128.49,
                           op0=AOP.min, op1=AOP.max, waits=[m1])
                m3 = P.op("ve", "tensor_scalar", r, r, 129.5, None,
                          op0=AOP.add, waits=[m2_])
                macc = P.op("ve", "memset", acc, -129.0, waits=[acc_tr_free])
                prev = m3
                for k in range(8, -1, -1):
                    th = float(2 ** k)
                    mb_ = P.op("ve", "tensor_scalar", bt, r, th, -th,
                               op0=AOP.is_ge, op1=AOP.mult, waits=[prev])
                    P.op("ve", "tensor_tensor", r, r, bt, op=AOP.add, waits=[mb_])
                    prev = P.op("ve", "tensor_tensor", acc, acc, bt, op=AOP.subtract)
                xt_free = m1
                last_ev = xq_ev_free
                for hc4 in range(0, 8, 4):
                    mts = None
                    for jj in range(4):
                        hc = hc4 + jj
                        mts = P.op("te", "transpose",
                                   psb[4][:, jj * 128:(jj + 1) * 128],
                                   acc[:, hc * 128:(hc + 1) * 128], ident[:],
                                   waits=[prev, macc, last_ev])
                    mcps = []
                    for jj in range(4):
                        hc = hc4 + jj
                        mcps.append(P.op("ve", "tensor_copy",
                                         xqT[:, hc * TQ + ti * 128:hc * TQ + (ti + 1) * 128],
                                         psb[4][:, jj * 128:(jj + 1) * 128],
                                         waits=[mts]))
                    last_ev = mcps[-1]
                    acc_tr_free = mts
                xq_ev_free = last_ev

            if debug:
                P.barrier()
                for hc in range(8):
                    mcv = P.op("ve", "tensor_copy", P_(6144, 2048),
                               xqT[:, hc * TQ:(hc + 1) * TQ], waits=[P.mark("c_gdbg")])
                    P.op("gp", "dma_start", sem="c_gdbg", amt=16,
                         out=dbg["d_xqi"][hc * 128:(hc + 1) * 128, :], in_=P_(6144, 2048),
                         waits=[mcv])
                mss = P.mark("ve")
                P.op("gp", "dma_start", sem="c_gdbg", amt=16, out=dbg["d_scal"][:, 0:2],
                     in_=sxv[:, 0:2], waits=[mss])

            P.barrier()

            msg = P.op("ve", "tensor_scalar", P_(4096, NIC), wsg_t[:, 0:NIC],
                       sxv[:, 1:2], None, op0=AOP.mult, waits=[mbx])
            msu = P.op("ve", "tensor_scalar", P_(4160, NIC), wsu_t[:, 0:NIC],
                       sxv[:, 1:2], None, op0=AOP.mult, waits=[msg])
            sgv_t = P_(4096, NIC)
            suv_t = P_(4160, NIC)
            mhab = P.op("ve", "memset", habs[:], 0.0)
            wq_free = [P.mark("te"), P.mark("te")]
            hp_free = [P.mark("c_ghp0"), P.mark("c_ghp1")]
            gs_free = P.mark("ve")
            for ic in range(NIC):
                pv = ic % 2
                wgsl = s_wgu[:, pv * 2048:pv * 2048 + 1024]
                wusl = s_wgu[:, pv * 2048 + 1024:pv * 2048 + 2048]
                mlg = P.op("sy", "dma_start", sem="c_w%d" % pv, amt=16,
                           out=wgsl.rearrange("p (a c) -> p a c", c=128),
                           in_=wgq_s[:, ic * 128:(ic + 1) * 128].rearrange(
                               "(a p) c -> p a c", p=128),
                           waits=[wq_free[pv]])
                mlu = P.op("sy", "dma_start", sem="c_w%d" % pv, amt=16,
                           out=wusl.rearrange("p (a c) -> p a c", c=128),
                           in_=wuq_s[:, ic * 128:(ic + 1) * 128].rearrange(
                               "(a p) c -> p a c", p=128))
                mlast = None
                for tb in range(4):
                    pbg = psb[0 + (tb % 2) * 2]
                    pbu = psb[1 + (tb % 2) * 2]
                    mg = None
                    mu = None
                    for fc in range(8):
                        wg_w = [mlu]
                        if fc == 0:
                            wg_w.append(gs_free if tb < 2 and ic == 0 else P.mark("sc"))
                        mg = P.op("te", "matmul", pbg[:, :],
                                  wgsl[:, fc * 128:(fc + 1) * 128],
                                  xqT[:, fc * TQ + tb * 512:fc * TQ + (tb + 1) * 512],
                                  waits=wg_w, start=(fc == 0), stop=(fc == 7),
                                  nopc=BF16_NOP)
                        mu = P.op("te", "matmul", pbu[:, :],
                                  wusl[:, fc * 128:(fc + 1) * 128],
                                  xqT[:, fc * TQ + tb * 512:fc * TQ + (tb + 1) * 512],
                                  waits=[mlu] + ([P.mark("ve")] if fc == 0 else []),
                                  start=(fc == 0), stop=(fc == 7), nopc=BF16_NOP)
                    gs = P_(2048, 512)
                    mgs = P.op("sc", "activation", out=gs, in_=pbg[:, :], func=AF.Silu,
                               scale=sgv_t[:, ic:ic + 1], waits=[mg, msg, gs_free])
                    hpt = P_(2560 + (tb % 2) * 512, 512)
                    mhp = P.op("ve", "scalar_tensor_tensor", hpt, pbu[:, :],
                               suv_t[:, ic:ic + 1], gs, op0=AOP.mult, op1=AOP.mult,
                               waits=[mu, mgs, msu, hp_free[tb % 2]])
                    gs_free = mhp
                    mlast = mhp
                    mrm = P.op("ve", "tensor_reduce", smal[:, 12:13], hpt, axis=AX.X,
                               op=AOP.max, apply_absolute_value=True, waits=[mhp])
                    mhab = P.op("ve", "tensor_tensor", habs[:], habs[:], smal[:, 12:13],
                                op=AOP.max, waits=[mrm, mhab])
                    hp_free[tb % 2] = P.op("gp", "dma_start", sem="c_ghp%d" % (tb % 2),
                                           amt=16,
                                           out=hp_s[ic * 128:(ic + 1) * 128,
                                                    tb * 512:(tb + 1) * 512],
                                           in_=hpt, waits=[mhp])
                wq_free[pv] = mlast

            if debug:
                P.barrier()
                for r in range(NIC):
                    mld = P.op("sy", "dma_start", sem="c_dbg", amt=16, out=P_(6144, 2048),
                               in_=hp_s[r * 128:(r + 1) * 128, :], waits=[P.mark("c_gdbg")])
                    P.op("gp", "dma_start", sem="c_gdbg", amt=16,
                         out=dbg["d_hp"][r * 128:(r + 1) * 128, :], in_=P_(6144, 2048),
                         waits=[mld])

            mcc_in2 = P.op("gp", "dma_start", sem="c_gcc", amt=16, out=cc2_in[:],
                           in_=habs[:].squeeze(1), waits=[mhab])
            mcc2 = P.op("gp", "collective_compute", "AllReduce", AOP.max,
                        sem="cc", amt=1, replica_groups=[list(range(8))],
                        ins=[cc2_in[:]], outs=[cc2_out[:]], waits=[mcc_in2])
            mccb2 = P.op("gp", "dma_start", sem="c_gcc", amt=16, out=P_(5221, 1),
                         in_=cc2_out[:].unsqueeze(1), waits=[mcc2])
            mgr2 = P.op("gp", "tensor_reduce", P_(5222, 1)[:1, :], P_(5221, 1),
                        sem="gpc", axis=AX.C, op=AOP.max, waits=[mccb2])
            msg2 = P.op("sc", "activation", out=smal[:1, 7:8], in_=smal[:1, 2:3],
                        func=AF.Sigmoid, waits=[m_ld0])
            msh = P.op("ve", "tensor_tensor", smal[:1, 7:8], smal[:1, 7:8],
                       P_(5222, 1)[:1, :], op=AOP.mult, waits=[msg2, mgr2])
            msh2 = P.op("ve", "tensor_scalar", smal[:1, 7:8], smal[:1, 7:8],
                        1.0 / QMAX, None, op0=AOP.mult, waits=[msh])
            mih = P.op("ve", "reciprocal", smal[:1, 8:9], smal[:1, 7:8], waits=[msh2])
            mbh = bcast_scalar(smal[:1, 7:8], sxv[:, 3:4], [msh2])
            mbih = bcast_scalar(smal[:1, 8:9], sxv[:, 2:3], [mih])
            mwf = P.op("ve", "tensor_scalar", wdsb[:, :], wdsb[:, :], sxv[:, 3:4],
                       None, op0=AOP.mult, waits=[mbh])

            P.barrier()

            hq_free = [P.mark("te"), P.mark("te")]
            wd_free = [P.mark("te"), P.mark("te")]
            st_free = [P.mark("c_go0"), P.mark("c_go1")]
            lad_free = P.mark("ve")
            for tb in range(4):
                mdown = [None] * 8
                for ic in range(NIC):
                    pv = ic % 2
                    hsl = P_(pv * 512, 512)
                    mlh = P.op("sy", "dma_start", sem="c_h%d" % pv, amt=16, out=hsl,
                               in_=hp_s[ic * 128:(ic + 1) * 128, tb * 512:(tb + 1) * 512],
                               waits=[lad_free])
                    wdsl = s_wd2[:, pv * 1024:(pv + 1) * 1024]
                    mlw = P.op("sy", "dma_start", sem="c_wd%d" % pv, amt=16, out=wdsl,
                               in_=wdq_s[ic * 128:(ic + 1) * 128, :],
                               waits=[wd_free[pv]])
                    r = P_(1024, 512)
                    acc = P_(1536, 512)
                    bt = P_(2048, 512)
                    m1 = P.op("ve", "tensor_scalar", r, hsl, sxv[:, 2:3], 127.49,
                              op0=AOP.mult, op1=AOP.min, waits=[mlh, mbih])
                    m2_ = P.op("ve", "tensor_scalar", r, r, -128.49, 129.5,
                               op0=AOP.max, op1=AOP.add, waits=[m1])
                    macc = P.op("ve", "memset", acc, -129.0)
                    prev = m2_
                    for k in range(8, -1, -1):
                        th = float(2 ** k)
                        mb_ = P.op("ve", "tensor_scalar", bt, r, th, -th,
                                   op0=AOP.is_ge, op1=AOP.mult, waits=[prev])
                        P.op("ve", "tensor_tensor", r, r, bt, op=AOP.add, waits=[mb_])
                        prev = P.op("ve", "tensor_tensor", acc, acc, bt, op=AOP.subtract)
                    lad_free = m1
                    hqsl = s_hq[:, pv * 512:(pv + 1) * 512]
                    mhq = P.op("ve", "tensor_copy", hqsl, acc,
                               waits=[prev, macc, hq_free[pv]])
                    if debug and tb == 0 and ic == 0:
                        mcvh = P.op("ve", "tensor_copy", P_(8192, 512), hqsl,
                                    waits=[mhq])
                        P.op("gp", "dma_start", sem="c_gdbg", amt=16,
                             out=dbg["d_hq"], in_=P_(8192, 512), waits=[mcvh])
                        mcvs = P.op("ve", "tensor_copy", P_(8704, 4), sxv[:, 0:4],
                                    waits=[mcvh])
                        P.op("gp", "dma_start", sem="c_gdbg", amt=16,
                             out=dbg["d_scal"][:, 2:6], in_=P_(8704, 4), waits=[mcvs])
                    mlast = None
                    for tt in range(4):
                        for oh in range(2):
                            pbi = tt * 2 + oh
                            w = [mhq, mlw]
                            if mdown[pbi] is not None:
                                w.append(mdown[pbi])
                            elif tb > 0:
                                w.append(P.mark("ve"))
                            mlast = P.op("te", "matmul", psb[pbi][:, :],
                                         hqsl[:, tt * 128:(tt + 1) * 128],
                                         wdsl[:, oh * 512:(oh + 1) * 512],
                                         waits=w, start=(ic == 0), stop=(ic == NIC - 1),
                                         nopc=BF16_NOP)
                            mdown[pbi] = mlast
                    hq_free[pv] = mlast
                    wd_free[pv] = mlast
                mxr_free = P.mark("ve")
                for tt in range(4):
                    mxr = P.op("sy", "dma_start", sem="c_xr%s" % ("" if tt % 2 == 0 else "2"),
                               amt=16,
                               out=P_(4096 + (tt % 2) * 1024, 1024),
                               in_=xp_s[tb * 512 + tt * 128:tb * 512 + (tt + 1) * 128, :],
                               waits=[mxr_free if tt < 2 else P.mark("ve")])
                    for oh in range(2):
                        pbi = tt * 2 + oh
                        pv = pbi % 2
                        stg = P_(6144 + pv * 512, 512)
                        m1 = P.op("ve", "tensor_tensor", stg, psb[pbi][:, :],
                                  wdsb[:, oh * 512:(oh + 1) * 512], op=AOP.mult,
                                  waits=[mdown[pbi], mwf, st_free[pv]])
                        m2_ = P.op("ve", "tensor_tensor", stg, stg,
                                   P_(4096 + (tt % 2) * 1024, 1024)[:, oh * 512:(oh + 1) * 512],
                                   op=AOP.add, waits=[m1, mxr])
                        st_free[pv] = P.op("gp", "dma_start", sem="c_go%d" % pv, amt=16,
                                           out=out[tb * 512 + tt * 128:
                                                   tb * 512 + (tt + 1) * 128,
                                                   oh * 512:(oh + 1) * 512],
                                           in_=stg, waits=[m2_])

            P.wait("gp", [P.mark("c_go0"), P.mark("c_go1")])

        handles_names = ["te", "sc", "ve", "gp", "sy"]

        def run_engine(name, h):
            hh = dict.fromkeys(handles_names, None)
            hh[name] = h
            schedule(Prog(hh, sems))

        @block.sync
        def _(sync):
            run_engine("sy", sync)

        @block.tensor
        def _(tensor):
            run_engine("te", tensor)

        @block.scalar
        def _(scalar):
            run_engine("sc", scalar)

        @block.vector
        def _(vector):
            run_engine("ve", vector)

        @block.gpsimd
        def _(gpsimd):
            run_engine("gp", gpsimd)

    return nc, dbg


def make_masks(p):
    import ml_dtypes
    m = np.zeros((4, 8, 128, 512), dtype=np.float32)
    stripes = STRIPES[p]
    for j in range(4):
        s = stripes[j]
        q0 = s * 512
        for w in range(8):
            kc = KCDM[j] - 8 + w
            kk = kc * 128 + np.arange(128)[:, None]
            qq = q0 + np.arange(512)[None, :]
            m[j, w] = (kk <= qq).astype(np.float32)
    return m.astype(ml_dtypes.bfloat16)


def kernel(L0_attn_x, L0_attn_W_attn, L0_attn_b_attn, L0_attn_W_proj,
           L0_attn_b_proj, L0_attn_score_gain, L0_ffn_clip_x, L0_ffn_clip_down,
           L0_ffn_w_gate, L0_ffn_w_up, L0_ffn_w_down):
    x = np.ascontiguousarray(np.asarray(L0_attn_x, dtype=np.float32))
    Wa_ = np.ascontiguousarray(np.asarray(L0_attn_W_attn, dtype=np.float32))
    ba_ = np.ascontiguousarray(np.asarray(L0_attn_b_attn, dtype=np.float32))
    Wp_ = np.ascontiguousarray(np.asarray(L0_attn_W_proj, dtype=np.float32))
    bp_ = np.ascontiguousarray(np.asarray(L0_attn_b_proj, dtype=np.float32))
    g_ = np.asarray(L0_attn_score_gain, dtype=np.float32).reshape(1)
    cx_ = np.asarray(L0_ffn_clip_x, dtype=np.float32).reshape(1)
    cd_ = np.asarray(L0_ffn_clip_down, dtype=np.float32).reshape(1)
    wg_ = np.ascontiguousarray(np.asarray(L0_ffn_w_gate, dtype=np.float32))
    wu_ = np.ascontiguousarray(np.asarray(L0_ffn_w_up, dtype=np.float32))
    wd_ = np.ascontiguousarray(np.asarray(L0_ffn_w_down, dtype=np.float32))
    ident = np.eye(128, dtype=np.float32)

    nc, _ = build_program(debug=False)
    in_maps = []
    for c in range(8):
        b, p = c // 2, c % 2
        rows = np.concatenate([np.arange(s * 512, (s + 1) * 512) for s in STRIPES[p]])
        xb = np.ascontiguousarray(x[b])
        in_maps.append({
            "x": xb, "xq": np.ascontiguousarray(xb[rows]),
            "Wa": Wa_, "ba": ba_, "Wp": Wp_, "bp": bp_,
            "gsc": g_, "clipx": cx_, "clipd": cd_,
            "wg": wg_, "wu": wu_, "wd": wd_,
            "ident": ident, "m01": make_masks(p),
        })
    res = run_bass_kernel_spmd(nc, in_maps, core_ids=list(range(8)))
    outp = np.empty((B, T, C), dtype=np.float32)
    for c in range(8):
        b, p = c // 2, c % 2
        o = res.results[c]["out"]
        for jj, s in enumerate(STRIPES[p]):
            outp[b, s * 512:(s + 1) * 512, :] = o[jj * 512:(jj + 1) * 512, :]
    return outp


# version 07
# speedup vs baseline: 1.0032x; 1.0032x over previous
import os
import numpy as np
import concourse.bass as bass
import concourse.mybir as mybir
from concourse.bass_utils import run_bass_kernel_spmd
from contextlib import ExitStack

_SIM = os.environ.get("BASS_SIM_FP64") == "1"

F32 = mybir.dt.float32
BF16 = mybir.dt.bfloat16
AOP = mybir.AluOpType
AF = mybir.ActivationFunctionType
AX = mybir.AxisListType

B, T, C = 4, 4096, 1024
NH, D = 16, 64
C3 = 3 * C
I = 2816
TQ = 2048
NTB = 8
NIC = I // 128
QMAX = 127.0
KCDM = [8, 16, 24, 32]
STRIPES = [[0, 3, 4, 7], [1, 2, 5, 6]]

BF16_NOP = 280
K64_NOP = 130
TR_NOP = 280


class Prog:
    def __init__(self, handles, sems):
        self.h = handles
        self.sems = sems
        self.cnt = {k: 0 for k in sems}
        self.lastwait = {}

    def wait(self, eng, marks):
        h = self.h[eng]
        for (s, v) in marks:
            if v <= 0:
                continue
            key = (eng, s)
            if self.lastwait.get(key, -1) >= v:
                continue
            self.lastwait[key] = v
            if h is not None:
                h.wait_ge(self.sems[s], v)

    def op(self, eng, meth, *a, waits=(), sem=None, amt=1, nopc=0, **kw):
        self.wait(eng, waits)
        h = self.h[eng]
        if sem is None:
            sem = eng
        self.cnt[sem] += amt
        if h is not None:
            ins = getattr(h, meth)(*a, **kw)
            ins.then_inc(self.sems[sem], amt)
            if nopc and not _SIM:
                h.nop(cycle_cnt=nopc)
        return (sem, self.cnt[sem])

    def mark(self, sem):
        return (sem, self.cnt[sem])

    def barrier(self):
        marks = [self.mark(s) for s in self.cnt]
        for eng in ("te", "sc", "ve", "gp", "sy"):
            self.wait(eng, marks)


def build_program(debug=False):
    nc = bass.Bass("TRN2", target_bir_lowering=False)

    x_in = nc.dram_tensor("x", [T, C], F32, kind="ExternalInput").ap()
    xq_in = nc.dram_tensor("xq", [TQ, C], F32, kind="ExternalInput").ap()
    Wa = nc.dram_tensor("Wa", [C, C3], F32, kind="ExternalInput").ap()
    ba = nc.dram_tensor("ba", [C3], F32, kind="ExternalInput").ap()
    Wp = nc.dram_tensor("Wp", [C, C], F32, kind="ExternalInput").ap()
    bp = nc.dram_tensor("bp", [C], F32, kind="ExternalInput").ap()
    gsc = nc.dram_tensor("gsc", [1], F32, kind="ExternalInput").ap()
    clipx = nc.dram_tensor("clipx", [1], F32, kind="ExternalInput").ap()
    clipd = nc.dram_tensor("clipd", [1], F32, kind="ExternalInput").ap()
    wg = nc.dram_tensor("wg", [I, C], F32, kind="ExternalInput").ap()
    wu = nc.dram_tensor("wu", [I, C], F32, kind="ExternalInput").ap()
    wd = nc.dram_tensor("wd", [C, I], F32, kind="ExternalInput").ap()
    ident_in = nc.dram_tensor("ident", [128, 128], F32, kind="ExternalInput").ap()
    m01_in = nc.dram_tensor("m01", [4, 8, 128, 512], BF16, kind="ExternalInput").ap()

    out = nc.dram_tensor("out", [TQ, C], F32, kind="ExternalOutput").ap()

    qT_s = nc.dram_tensor("qT_s", [C, TQ], F32).ap()
    kT_s = nc.dram_tensor("kT_s", [C, T], F32).ap()
    vT_s = nc.dram_tensor("vT_s", [T, C], F32).ap()
    yT_s = nc.dram_tensor("yT_s", [C, TQ], F32).ap()
    xp_s = nc.dram_tensor("xp_s", [TQ, C], F32).ap()
    hp_s = nc.dram_tensor("hp_s", [I, TQ], F32).ap()
    km_s = nc.dram_tensor("km_s", [128, 8], F32).ap()
    wgq_s = nc.dram_tensor("wgq_s", [C, I], BF16).ap()
    wuq_s = nc.dram_tensor("wuq_s", [C, I], BF16).ap()
    wdq_s = nc.dram_tensor("wdq_s", [I, C], BF16).ap()
    cc1_in = nc.dram_tensor("cc1_in", [128], F32).ap()
    cc1_out = nc.dram_tensor("cc1_out", [128], F32, addr_space="Shared").ap()
    cc2_in = nc.dram_tensor("cc2_in", [128], F32).ap()
    cc2_out = nc.dram_tensor("cc2_out", [128], F32, addr_space="Shared").ap()

    dbg = {}
    if debug:
        for nm, shp in [("d_qT", [C, TQ]), ("d_kT", [C, T]), ("d_vT", [T, C]),
                        ("d_yT", [C, TQ]), ("d_xp", [TQ, C]), ("d_hp", [I, TQ]),
                        ("d_scal", [128, 16]), ("d_xqi", [C, TQ]),
                        ("d_e", [128, 512]), ("d_q2", [128, 512]),
                        ("d_u4", [128, 512]), ("d_nrm", [128, 8]),
                        ("d_wdq", [I, C]), ("d_hq", [128, 512])]:
            dbg[nm] = nc.dram_tensor(nm, shp, F32, kind="ExternalOutput").ap()

    with ExitStack() as ctx:
        sbt = lambda name, shape, dt=F32: ctx.enter_context(nc.sbuf_tensor(name, shape, dt))

        ident = sbt("s_ident", [128, 128])
        identb = sbt("s_identb", [128, 128], BF16)
        ones = sbt("s_ones", [128, 130])
        ba_t = sbt("s_ba", [128, 24])
        wsg_t = sbt("s_wsg", [128, NIC])
        wsu_t = sbt("s_wsu", [128, NIC])
        wsd_t = sbt("s_wsd", [128, 8])
        wdsb = sbt("s_wdsb", [128, C])
        kmean = sbt("s_kmean", [128, 8])
        ksump = sbt("s_ksump", [128, 64])
        smal = sbt("s_smal", [128, 48])
        invg = sbt("s_invg", [128, 1])
        habs = sbt("s_habs", [128, 1])
        sxv = sbt("s_sxv", [128, 4])
        xqT = sbt("s_xqT", [128, 8 * TQ], BF16)
        mtiles = sbt("s_mt", [128, 8 * 512], BF16)
        pool = sbt("s_pool", [128, 24576], F32)
        s_wstage = sbt("s_wstage", [128, 512], BF16)
        s_wgu = sbt("s_wgu", [128, 4096], BF16)
        s_wd2 = sbt("s_wd2", [128, 2048], BF16)
        s_hq = sbt("s_hq", [128, 1024], BF16)
        psb = [ctx.enter_context(nc.psum_tensor(f"pb{i}", [128, 512], F32)) for i in range(8)]

        sem_names = ["te", "sc", "ve", "gpc", "cc", "dout",
                     "c_ini", "c_w0", "c_w1", "c_x", "c_wa0", "c_wa1",
                     "c_vw0", "c_vw1", "c_dbg", "c_msk", "c_km", "c_k",
                     "c_q0", "c_q1", "c_v0", "c_v1", "c_wp", "c_xr",
                     "c_y0", "c_y1", "c_h0", "c_h1", "c_wd0", "c_wd1",
                     "c_gst", "c_gkq0", "c_gkq1", "c_gv0", "c_gv1",
                     "c_gkm", "c_gyt", "c_gxp0", "c_gxp1", "c_ghp0",
                     "c_ghp1", "c_gcc", "c_gdbg", "c_go0", "c_go1", "c_xr2"]
        sems = {n: ctx.enter_context(nc.semaphore("sem_" + n)) for n in sem_names}
        block = ctx.enter_context(nc.Block())

        def P_(a, n):
            assert a + n <= 24576, (a, n)
            return pool[:, a:a + n]

        def schedule(P: Prog):
            P.op("sy", "dma_start", sem="c_ini", amt=16, out=ident[:], in_=ident_in)
            P.op("sy", "dma_start", sem="c_ini", amt=16, out=ba_t[:, 0:24],
                 in_=ba.rearrange("(c p) -> p c", p=128), allow_slow_non_contiguous=True)
            P.op("sy", "dma_start", sem="c_ini", amt=16, out=smal[:1, 0:1], in_=gsc.unsqueeze(0))
            P.op("sy", "dma_start", sem="c_ini", amt=16, out=smal[:1, 1:2], in_=clipx.unsqueeze(0))
            P.op("sy", "dma_start", sem="c_ini", amt=16, out=smal[:1, 2:3], in_=clipd.unsqueeze(0))
            m_ld0 = P.mark("c_ini")
            m_ones = P.op("ve", "memset", ones[:], 1.0)
            m_idb = P.op("ve", "tensor_copy", identb[:], ident[:], waits=[m_ld0])
            P.op("ve", "memset", habs[:], 0.0)
            P.op("ve", "memset", pool[:, 0:512], 0.0)

            m_warm = P.op("te", "matmul", psb[7][:1, :], ones[:, 0:1], pool[:, 0:512],
                          waits=[m_ones, P.mark("ve")], start=True, stop=True)

            m_rg = P.op("ve", "reciprocal", smal[:1, 4:5], smal[:1, 0:1], waits=[m_ld0])
            m_bg = P.op("te", "matmul", psb[6][:, 0:1], ones[:1, 0:128], smal[:1, 4:5],
                        waits=[m_rg, m_warm], start=True, stop=True)
            m_invg = P.op("ve", "tensor_copy", invg[:], psb[6][:, 0:1], waits=[m_bg])

            def bcast_scalar(src_ap, dst_ap, waits):
                mm = P.op("te", "matmul", psb[6][:, 4:5], ones[:1, 0:128], src_ap,
                          waits=list(waits) + [P.mark("ve")], start=True, stop=True)
                return P.op("ve", "tensor_copy", dst_ap, psb[6][:, 4:5], waits=[mm])

            def ladder_w(src_ap, n, invs_ap, acc_waits, waits):
                r = P_(4096, n)
                acc = P_(6144, n)
                bt = P_(8192, n)
                m1 = P.op("ve", "tensor_scalar", r, src_ap, invs_ap, 127.49,
                          op0=AOP.mult, op1=AOP.min, waits=list(waits))
                m2 = P.op("ve", "tensor_scalar", r, r, -128.49, 129.5,
                          op0=AOP.max, op1=AOP.add, waits=[m1])
                m3 = P.op("ve", "memset", acc, -129.0, waits=list(acc_waits))
                prev = m2
                for k in range(8, -1, -1):
                    th = float(2 ** k)
                    mb = P.op("ve", "tensor_scalar", bt, r, th, -th,
                              op0=AOP.is_ge, op1=AOP.mult, waits=[prev])
                    P.op("ve", "tensor_tensor", r, r, bt, op=AOP.add, waits=[mb])
                    prev = P.op("ve", "tensor_tensor", acc, acc, bt, op=AOP.subtract)
                return prev, P_(6144, n)

            def transpose_out(ib, ncc, dst_fn, waits):
                last_ev = P.mark("ve")
                last_dma = P.mark("c_gst")
                last_t = None
                for cc4 in range(0, ncc, 4):
                    nblk = min(4, ncc - cc4)
                    mts = None
                    for jj in range(nblk):
                        cc = cc4 + jj
                        mts = P.op("te", "transpose",
                                   psb[4][:, jj * 128:(jj + 1) * 128],
                                   ib[:, cc * 128:(cc + 1) * 128], ident[:],
                                   waits=list(waits) + [last_ev])
                    stage = s_wstage
                    mev = P.op("ve", "tensor_copy", stage[:, 0:nblk * 128],
                               psb[4][:, 0:nblk * 128],
                               waits=[mts, last_dma])
                    last_ev = mev
                    last_t = mts
                    for jj in range(nblk):
                        last_dma = P.op("gp", "dma_start", sem="c_gst", amt=16,
                                        out=dst_fn(cc4 + jj),
                                        in_=stage[:, jj * 128:(jj + 1) * 128],
                                        waits=[mev])
                return last_t

            def quant_gateup(wmat, scale_tile, outT):
                nonlocal acc_free
                buf_free = [P.mark("ve"), P.mark("ve")]
                for rc in range(NIC):
                    par = rc % 2
                    wbuf = P_(par * 2048, 1024)
                    mld = P.op("sy", "dma_start", sem="c_w%d" % par, amt=16,
                               out=wbuf, in_=wmat[rc * 128:(rc + 1) * 128, :],
                               waits=[buf_free[par]])
                    mrm = P.op("ve", "tensor_reduce", smal[:, 8:9], wbuf,
                               axis=AX.X, op=AOP.max, apply_absolute_value=True,
                               waits=[mld])
                    msc = P.op("ve", "tensor_scalar", scale_tile[:, rc:rc + 1],
                               smal[:, 8:9], 1.0 / QMAX, None, op0=AOP.mult, waits=[mrm])
                    miv = P.op("ve", "reciprocal", smal[:, 9:10],
                               scale_tile[:, rc:rc + 1], waits=[msc])
                    mq, accv = ladder_w(wbuf, 1024, smal[:, 9:10], [acc_free], [miv])
                    buf_free[par] = mq
                    acc_free = transpose_out(accv, 8,
                                             lambda cc, rc=rc: outT[cc * 128:(cc + 1) * 128,
                                                                    rc * 128:(rc + 1) * 128],
                                             [mq])

            acc_free = P.mark("te")
            quant_gateup(wg, wsg_t, wgq_s)
            quant_gateup(wu, wsu_t, wuq_s)

            for rc in range(8):
                wbuf0 = P_(0, 1408)
                wbuf1 = P_(2048, 1408)
                mld0 = P.op("sy", "dma_start", sem="c_w0", amt=16, out=wbuf0,
                            in_=wd[rc * 128:(rc + 1) * 128, 0:1408],
                            waits=[P.mark("ve")])
                mld1 = P.op("sy", "dma_start", sem="c_w1", amt=16, out=wbuf1,
                            in_=wd[rc * 128:(rc + 1) * 128, 1408:2816])
                mrm0 = P.op("ve", "tensor_reduce", smal[:, 8:9], wbuf0,
                            axis=AX.X, op=AOP.max, apply_absolute_value=True, waits=[mld0])
                mrm1 = P.op("ve", "tensor_reduce", smal[:, 10:11], wbuf1,
                            axis=AX.X, op=AOP.max, apply_absolute_value=True, waits=[mld1])
                mrm = P.op("ve", "tensor_tensor", smal[:, 8:9], smal[:, 8:9],
                           smal[:, 10:11], op=AOP.max, waits=[mrm0, mrm1])
                msc = P.op("ve", "tensor_scalar", wsd_t[:, rc:rc + 1], smal[:, 8:9],
                           1.0 / QMAX, None, op0=AOP.mult, waits=[mrm])
                miv = P.op("ve", "reciprocal", smal[:, 9:10], wsd_t[:, rc:rc + 1],
                           waits=[msc])
                for half in range(2):
                    c0 = half * 1408
                    wbuf = wbuf0 if half == 0 else wbuf1
                    mq, accv = ladder_w(wbuf, 1408, smal[:, 9:10], [acc_free], [miv])
                    acc_free = transpose_out(accv, 11,
                                             lambda cc, c0=c0, rc=rc: wdq_s[
                                                 c0 + cc * 128:c0 + (cc + 1) * 128,
                                                 rc * 128:(rc + 1) * 128], [mq])

            m_wsd = P.mark("ve")
            mts = None
            for j in range(8):
                bank = psb[4] if j < 4 else psb[5]
                jj = j % 4
                mts = P.op("te", "transpose", bank[:1, jj * 128:(jj + 1) * 128],
                           wsd_t[:, j:j + 1], ident[:], waits=[m_wsd],
                           nopc=TR_NOP)
            mrow0 = P.op("ve", "tensor_copy", P_(11776, 1024)[:1, 0:512],
                         psb[4][:1, 0:512], waits=[mts])
            mrow = P.op("ve", "tensor_copy", P_(11776, 1024)[:1, 512:1024],
                        psb[5][:1, 0:512], waits=[mts])
            for hh in range(2):
                mb = P.op("te", "matmul", psb[6][:, :], ones[:1, 0:128],
                          P_(11776, 1024)[:1, hh * 512:(hh + 1) * 512],
                          waits=[mrow, P.mark("ve")], start=True, stop=True)
                P.op("ve", "tensor_copy", wdsb[:, hh * 512:(hh + 1) * 512], psb[6][:, :],
                     waits=[mb])
            P.barrier()
            if debug:
                for r in range(NIC):
                    mldw = P.op("sy", "dma_start", sem="c_dbg", amt=16,
                                out=P_(0, 512).bitcast(BF16)[:, 0:1024],
                                in_=wdq_s[r * 128:(r + 1) * 128, :],
                                waits=[P.mark("c_gdbg"), P.mark("ve")])
                    mcvw = P.op("ve", "tensor_copy", P_(1024, 1024),
                                P_(0, 512).bitcast(BF16)[:, 0:1024], waits=[mldw])
                    P.op("gp", "dma_start", sem="c_gdbg", amt=16,
                         out=dbg["d_wdq"][r * 128:(r + 1) * 128, :], in_=P_(1024, 1024),
                         waits=[mcvw])
                P.barrier()

            bvb = P_(20480, 1024)
            mbv = P.op("sy", "dma_start", sem="c_ini", amt=16, out=bvb[:1, :],
                       in_=ba[2 * C:3 * C].unsqueeze(0))
            for hh in range(2):
                mb = P.op("te", "matmul", psb[6][:, :], ones[:1, 0:128],
                          bvb[:1, hh * 512:(hh + 1) * 512],
                          waits=[mbv, P.mark("ve")], start=True, stop=True)
                P.op("ve", "tensor_copy", bvb[:, hh * 512:(hh + 1) * 512], psb[6][:, :],
                     waits=[mb])
            m_bvb = P.mark("ve")

            def ln_tile(xt_ap, waits):
                sq = P_(4096, 1024)
                mS = P.op("ve", "tensor_reduce", smal[:, 12:13], xt_ap, axis=AX.X,
                          op=AOP.add, waits=list(waits))
                mM = P.op("ve", "tensor_scalar", smal[:, 12:13], smal[:, 12:13],
                          1.0 / C, None, op0=AOP.mult, waits=[mS])
                mQ = P.op("ve", "tensor_tensor", sq, xt_ap, xt_ap, op=AOP.mult,
                          waits=list(waits))
                mS2 = P.op("ve", "tensor_reduce", smal[:, 13:14], sq, axis=AX.X,
                           op=AOP.add, waits=[mQ])
                mE2 = P.op("ve", "tensor_scalar", smal[:, 13:14], smal[:, 13:14],
                           1.0 / C, None, op0=AOP.mult, waits=[mS2])
                mMM = P.op("ve", "tensor_tensor", smal[:, 14:15], smal[:, 12:13],
                           smal[:, 12:13], op=AOP.mult, waits=[mM])
                mV = P.op("ve", "tensor_tensor", smal[:, 13:14], smal[:, 13:14],
                          smal[:, 14:15], op=AOP.subtract, waits=[mE2, mMM])
                mVe = P.op("ve", "tensor_scalar", smal[:, 13:14], smal[:, 13:14],
                           1e-5, None, op0=AOP.add, waits=[mV])
                mR = P.op("ve", "reciprocal", smal[:, 13:14], smal[:, 13:14], waits=[mVe])
                mRs = P.op("sc", "activation", out=smal[:, 15:16], in_=smal[:, 13:14],
                           func=AF.Sqrt, waits=[mR])
                return P.op("ve", "tensor_scalar", xt_ap, xt_ap, smal[:, 12:13],
                            smal[:, 15:16], op0=AOP.subtract, op1=AOP.mult,
                            waits=[mRs])

            def transpose_xh(xt_tiles_marks):
                marks = []
                last_ev = P.mark("ve")
                for fc in range(8):
                    mts = None
                    for tt in range(4):
                        mts = P.op("te", "transpose", psb[4][:, tt * 128:(tt + 1) * 128],
                                   P_(tt * 1024, 1024)[:, fc * 128:(fc + 1) * 128],
                                   ident[:],
                                   waits=[xt_tiles_marks[tt], last_ev])
                    mev = P.op("ve", "tensor_copy", P_(5120 + fc * 512, 512),
                               psb[4][:, :], waits=[mts])
                    last_ev = mev
                    marks.append(mev)
                return marks

            wa_free = [P.mark("te"), P.mark("te")]
            kq_stage_free = [P.mark("c_gkq0"), P.mark("c_gkq1")]
            v_stage_free = [P.mark("c_gv0"), P.mark("c_gv1")]
            vw_free = [P.mark("te"), P.mark("te")]
            v_ps_free = [P.mark("ve"), P.mark("ve")]
            m_abs_psA = [P.mark("sc"), P.mark("sc")]

            def qkv_pass(src_ap, tb, do_q, do_kv, tb_cols):
                mx = []
                for tt in range(4):
                    mx.append(P.op("sy", "dma_start", sem="c_x", amt=16,
                                   out=P_(tt * 1024, 1024),
                                   in_=src_ap[tb * 512 + tt * 128:tb * 512 + (tt + 1) * 128, :],
                                   waits=[P.mark("te"), P.mark("ve")]))
                mln = []
                prev = P.mark("ve")
                for tt in range(4):
                    m = ln_tile(P_(tt * 1024, 1024), [mx[3], prev])
                    mln.append(m)
                    prev = m
                mT = transpose_xh(mln)
                chunks = (list(range(8)) if do_q else []) + (list(range(8, 16)) if do_kv else [])
                for cchunk in chunks:
                    pv = cchunk % 2
                    pbk = psb[0 + pv]
                    mmm = None
                    for fc in range(8):
                        mmm = P.op("te", "matmul", pbk[:, :],
                                   P_(9216 + fc * 1024, 1024)[:, (cchunk % 8) * 128:
                                                              (cchunk % 8) * 128 + 128],
                                   P_(5120 + fc * 512, 512),
                                   waits=[m_wk, mT[fc], m_abs_psA[pv]],
                                   start=(fc == 0), stop=(fc == 7))
                    stg = P_(18432 + pv * 512, 512)
                    if cchunk >= 8:
                        kc_i = cchunk - 8
                        mev = P.op("sc", "activation", out=stg, in_=pbk[:, :],
                                   func=AF.Identity, bias=ba_t[:, cchunk:cchunk + 1],
                                   accum_out=ksump[:, kc_i * 8 + tb:kc_i * 8 + tb + 1],
                                   waits=[mmm, kq_stage_free[pv]])
                        dst = kT_s[kc_i * 128:(kc_i + 1) * 128, tb_cols]
                    else:
                        mev = P.op("sc", "activation", out=stg, in_=pbk[:, :],
                                   func=AF.Identity, bias=ba_t[:, cchunk:cchunk + 1],
                                   waits=[mmm, kq_stage_free[pv]])
                        dst = qT_s[cchunk * 128:(cchunk + 1) * 128, tb_cols]
                    m_abs_psA[pv] = mev
                    kq_stage_free[pv] = P.op("gp", "dma_start", sem="c_gkq%d" % pv,
                                             amt=16, out=dst, in_=stg, waits=[mev])
                if do_kv:
                    for tt in range(4):
                        for vh in range(2):
                            pv = (tt * 2 + vh) % 2
                            pbv = psb[2 + pv]
                            wslot = P_(17408 + pv * 512, 512)
                            mmm = None
                            for fc in range(8):
                                mws = P.op("sy", "dma_start", sem="c_vw%d" % pv, amt=16,
                                           out=wslot,
                                           in_=Wa[fc * 128:(fc + 1) * 128,
                                                  2 * C + vh * 512:2 * C + (vh + 1) * 512],
                                           waits=[vw_free[pv], P.mark("te")])
                                mmm = P.op("te", "matmul", pbv[:, :],
                                           P_(5120 + fc * 512, 512)[:, tt * 128:(tt + 1) * 128],
                                           wslot,
                                           waits=[mws, mT[fc], v_ps_free[pv]],
                                           start=(fc == 0), stop=(fc == 7))
                            vw_free[pv] = mmm
                            stg = P_(19456 + pv * 512, 512)
                            mev = P.op("ve", "tensor_tensor", stg, pbv[:, :],
                                       bvb[:, vh * 512:(vh + 1) * 512], op=AOP.add,
                                       waits=[mmm, m_bvb, v_stage_free[pv]])
                            v_ps_free[pv] = mev
                            v_stage_free[pv] = P.op(
                                "gp", "dma_start", sem="c_gv%d" % pv, amt=16,
                                out=vT_s[tb * 512 + tt * 128:tb * 512 + (tt + 1) * 128,
                                         vh * 512:(vh + 1) * 512],
                                in_=stg, waits=[mev])

            m_wk = None
            for fc in range(8):
                m_wk = P.op("sy", "dma_start", sem="c_wa0", amt=16,
                            out=P_(9216 + fc * 1024, 1024),
                            in_=Wa[fc * 128:(fc + 1) * 128, C:2 * C],
                            waits=[P.mark("ve"), P.mark("te")])
            for tb in range(NTB):
                qkv_pass(x_in, tb, False, True, slice(tb * 512, (tb + 1) * 512))
            for fc in range(8):
                m_wk = P.op("sy", "dma_start", sem="c_wa0", amt=16,
                            out=P_(9216 + fc * 1024, 1024),
                            in_=Wa[fc * 128:(fc + 1) * 128, 0:C],
                            waits=[P.mark("te")])
            for tb in range(4):
                qkv_pass(xq_in, tb, True, False, slice(tb * 512, (tb + 1) * 512))

            mkm = []
            for kc_i in range(8):
                mk1 = P.op("ve", "tensor_reduce", kmean[:, kc_i:kc_i + 1],
                           ksump[:, kc_i * 8:(kc_i + 1) * 8], axis=AX.X, op=AOP.add,
                           waits=[P.mark("sc")])
                mkm.append(P.op("ve", "tensor_scalar", kmean[:, kc_i:kc_i + 1],
                                kmean[:, kc_i:kc_i + 1], 1.0 / T, None,
                                op0=AOP.mult, waits=[mk1]))
            m_km = P.op("gp", "dma_start", sem="c_gkm", amt=16, out=km_s[:, :],
                        in_=kmean[:, 0:8], waits=mkm)

            if debug:
                P.barrier()
                mkd = P.op("ve", "tensor_copy", P_(8192, 8), kmean[:, 0:8],
                           waits=[m_km])
                P.op("gp", "dma_start", sem="c_gdbg", amt=16, out=dbg["d_scal"][:, 8:16],
                     in_=P_(8192, 8), waits=[mkd])
                for r in range(8):
                    mld = P.op("sy", "dma_start", sem="c_dbg", amt=16, out=P_(0, 2048),
                               in_=qT_s[r * 128:(r + 1) * 128, :],
                               waits=[P.mark("c_gdbg"), P.mark("ve")])
                    P.op("gp", "dma_start", sem="c_gdbg", amt=16,
                         out=dbg["d_qT"][r * 128:(r + 1) * 128, :], in_=P_(0, 2048),
                         waits=[mld])
                for r in range(8):
                    mld = P.op("sy", "dma_start", sem="c_dbg", amt=16, out=P_(0, 4096),
                               in_=kT_s[r * 128:(r + 1) * 128, :], waits=[P.mark("c_gdbg")])
                    P.op("gp", "dma_start", sem="c_gdbg", amt=16,
                         out=dbg["d_kT"][r * 128:(r + 1) * 128, :], in_=P_(0, 4096),
                         waits=[mld])
                for r in range(32):
                    mld = P.op("sy", "dma_start", sem="c_dbg", amt=16, out=P_(0, 1024),
                               in_=vT_s[r * 128:(r + 1) * 128, :], waits=[P.mark("c_gdbg")])
                    P.op("gp", "dma_start", sem="c_gdbg", amt=16,
                         out=dbg["d_vT"][r * 128:(r + 1) * 128, :], in_=P_(0, 1024),
                         waits=[mld])


            P.barrier()

            kbuf = P_(0, 4096)
            ascr = P_(17920, 512)
            madp = P_(18432, 32)
            mad4 = P_(18464, 4)
            wch1 = P_(18468, 4)
            wch2 = P_(18472, 4)
            s4p = P_(18476, 4)
            wden = P_(18480, 4)
            w4 = P_(18484, 4)
            s4row = P_(18496, 512)
            wrow = P_(19008, 512)
            yt = P_(19520, 512)
            kmcur = P_(20032, 1)
            wbt = P_(20096, 512)

            mvo = []
            for hp in range(2):
                vb = P_(5120 + hp * 4096, 4096).rearrange("p (k c) -> p k c", c=128)
                m1_ = P.op("ve", "memset", vb[:, :, 0:1], 1.0)
                mvo.append(P.op("ve", "memset", vb[:, :, 1:64], 0.0, waits=[m1_]))

            km_free = P.mark("ve")
            bband_free = [P.mark("ve"), P.mark("ve"), P.mark("ve")]
            kbuf_free = [P.mark("te"), P.mark("ve")]
            q_free = [P.mark("te"), P.mark("te")]
            v_free = [mvo[0], mvo[1]]
            u4_free = [P.mark("te"), P.mark("te"), P.mark("te")]
            mask_free = P.mark("ve")
            yt_free = P.mark("c_gyt")
            s4_free = P.mark("te")
            s5_free = P.mark("ve")
            wbt_free = P.mark("ve")

            def emit_vload(h_):
                hp_ = h_ % 2
                vslot_ = P_(5120 + hp_ * 4096, 4096)
                return P.op("gp", "dma_start", sem="c_v%d" % hp_, amt=16,
                            out=vslot_.rearrange("p (k c) -> p k c", c=128)[:, :, 64:128],
                            in_=vT_s.rearrange("(k p) c -> p k c",
                                               p=128)[:, :, h_ * 64:(h_ + 1) * 64],
                            waits=[v_free[hp_]])

            mvl_marks = [None] * NH
            mvl_marks[0] = emit_vload(0)
            mvl_marks[1] = emit_vload(1)

            for h in range(NH):
                hp = h % 2
                mkm_l = P.op("sy", "dma_start", sem="c_km", amt=16,
                             out=kmcur[:64, :],
                             in_=km_s[(h % 2) * 64:(h % 2) * 64 + 64,
                                      h // 2:h // 2 + 1],
                             waits=[km_free, m_km],
                             allow_slow_non_contiguous=True)
                mkl = P.op("sy", "dma_start", sem="c_k", amt=16,
                           out=kbuf[:64, :],
                           in_=kT_s[h * 64:h * 64 + 64, :],
                           waits=kbuf_free)
                mkc = P.op("ve", "tensor_scalar", kbuf[:64, :], kbuf[:64, :],
                           kmcur[:64, :], None, op0=AOP.subtract,
                           waits=[mkl, mkm_l])
                km_free = mkc
                vslot = P_(5120 + hp * 4096, 4096)
                mvl = mvl_marks[h]
                for j in range(4):
                    kcd = KCDM[j]
                    mmask = P.op("sy", "dma_start", sem="c_msk", amt=16,
                                 out=mtiles.rearrange("p (a c) -> p a c", c=512),
                                 in_=m01_in[j].transpose([1, 0, 2]),
                                 waits=[mask_free])
                    qsl = P_(4096 + (j % 2) * 512, 512)
                    mql = P.op("sy", "dma_start", sem="c_q%d" % (j % 2), amt=16,
                               out=qsl[:64, :],
                               in_=qT_s[h * 64:h * 64 + 64, j * 512:(j + 1) * 512],
                               waits=[q_free[j % 2]])

                    mabs = [None, None]
                    last_abs = P.mark("sc")
                    a_steps = [(qt, kb) for qt in range(4) for kb in range(8)]
                    a_pos = 0

                    def emit_a(nsteps):
                        nonlocal a_pos, last_abs
                        for _ in range(nsteps):
                            if a_pos >= len(a_steps):
                                return
                            qt, kb = a_steps[a_pos]
                            pa = a_pos % 2
                            a_pos += 1
                            pba = psb[0 + pa]
                            w = [mql, mkc]
                            if mabs[pa] is not None:
                                w.append(mabs[pa])
                            mA = P.op("te", "matmul", pba[:, :],
                                      qsl[:64, qt * 128:(qt + 1) * 128],
                                      kbuf[:64, kb * 512:(kb + 1) * 512],
                                      waits=w, start=True, stop=True,
                                      nopc=K64_NOP)
                            mabs[pa] = P.op("sc", "activation", out=ascr, in_=pba[:, :],
                                            func=AF.Abs,
                                            accum_out=madp[:, qt * 8 + kb:qt * 8 + kb + 1],
                                            waits=[mA, last_abs])
                            last_abs = mabs[pa]

                    mq2 = [None, None, None]
                    mu4 = [None, None, None]
                    mav = None
                    bbanks = [2, 3, 7]
                    for kc in range(kcd):
                        pb_ = kc % 3
                        pbs = psb[bbanks[pb_]]
                        etile = P_(13312 + pb_ * 512, 512)
                        q2t = P_(14848 + pb_ * 512, 512)
                        u4t = P_(16384 + pb_ * 512, 512)
                        w = [mkc, mql]
                        if mq2[pb_] is not None:
                            w.append(mq2[pb_])
                        else:
                            w.append(bband_free[pb_])
                        mB = P.op("te", "matmul", pbs[:, :],
                                  kbuf[:64, kc * 128:(kc + 1) * 128],
                                  qsl[:64, :],
                                  waits=w, start=True, stop=True, nopc=K64_NOP)
                        emit_a(2 if kcd <= 16 else 1)
                        woff = kc - (kcd - 8)
                        we = [mB]
                        if mq2[pb_] is not None:
                            we.append(mq2[pb_])
                        if woff >= 0:
                            me = P.op("ve", "tensor_tensor", etile, pbs[:, :],
                                      mtiles.rearrange("p (a c) -> p a c", c=512)[:, woff, :],
                                      op=AOP.mult, waits=[mmask] + we)
                        else:
                            me = P.op("ve", "tensor_copy", etile, pbs[:, :], waits=we)
                        mq2[pb_] = P.op("ve", "scalar_tensor_tensor", q2t, pbs[:, :],
                                        0.0, etile, op0=AOP.max, op1=AOP.mult,
                                        waits=[me])
                        if debug and j == 0 and h == 0 and kc == 0:
                            P.op("gp", "dma_start", sem="c_gdbg", amt=16,
                                 out=dbg["d_e"], in_=etile, waits=[mq2[pb_]])
                            P.op("gp", "dma_start", sem="c_gdbg", amt=16,
                                 out=dbg["d_q2"], in_=q2t, waits=[mq2[pb_]])
                        mu4[pb_] = P.op("sc", "activation", out=u4t, in_=q2t,
                                        func=AF.Square,
                                        waits=[mq2[pb_], u4_free[pb_]])
                        if debug and j == 0 and h == 0 and kc == 0:
                            P.op("gp", "dma_start", sem="c_gdbg", amt=16,
                                 out=dbg["d_u4"], in_=u4t, waits=[mu4[pb_]])
                        mav = P.op("te", "matmul", psb[4][:, :],
                                   vslot.rearrange("p (k c) -> p k c", c=128)[:, kc, :],
                                   u4t,
                                   waits=[mu4[pb_], mvl, s4_free,
                                          P.mark("c_gdbg") if debug else ("te", 0)],
                                   start=(kc == 0), stop=(kc == kcd - 1))
                        u4_free[pb_] = mav
                    emit_a(len(a_steps))
                    kbuf_free = [mabs[0], mabs[1], mav, mkc]
                    q_free[j % 2] = mav
                    mask_free = P.mark("ve")
                    mmadr = []
                    for qt in range(4):
                        mmadr.append(P.op("ve", "tensor_reduce", mad4[:, qt:qt + 1],
                                          madp[:, qt * 8:(qt + 1) * 8], axis=AX.X,
                                          op=AOP.add, waits=[mabs[0], mabs[1]]))
                    mmad = P.op("ve", "tensor_scalar", mad4, mad4, 1.0 / T, 1e-6,
                                op0=AOP.mult, op1=AOP.add, waits=mmadr)
                    mw1 = P.op("ve", "tensor_scalar", wch1, mad4, invg[:, 0:1], None,
                               op0=AOP.mult, waits=[mmad, m_invg])
                    mw2 = P.op("ve", "tensor_tensor", wch2, wch1, wch1, op=AOP.mult,
                               waits=[mw1])
                    mw3 = P.op("ve", "tensor_tensor", wch1, wch2, wch2, op=AOP.mult,
                               waits=[mw2])
                    mw4 = P.op("ve", "tensor_scalar", wch1, wch1, 1e-6, None,
                               op0=AOP.mult, waits=[mw3])

                    ms4 = P.op("ve", "tensor_copy", s4row[:1, :], psb[4][0:1, :],
                               waits=[mav])
                    mt4 = None
                    for qt in range(4):
                        mt4 = P.op("te", "transpose", psb[5][:, qt:qt + 1],
                                   s4row[:1, qt * 128:(qt + 1) * 128], ident[:1, :1],
                                   waits=[ms4, s5_free])
                    ms4p = P.op("ve", "tensor_copy", s4p, psb[5][:, 0:4], waits=[mt4])
                    mden = P.op("ve", "tensor_tensor", wden, s4p, wch1, op=AOP.add,
                                waits=[ms4p, mw4])
                    mw4v = P.op("ve", "reciprocal", w4, wden, waits=[mden])
                    mtw = None
                    for qt in range(4):
                        mtw = P.op("te", "transpose", psb[5][:1, qt * 128:(qt + 1) * 128],
                                   w4[:, qt:qt + 1], ident[:],
                                   waits=[mw4v, ms4p])
                    mwr = P.op("ve", "tensor_copy", wrow[:1, :], psb[5][:1, 0:512],
                               waits=[mtw])
                    mwb = P.op("te", "matmul", psb[6][:, :], ones[:1, 0:128],
                               wrow[:1, :], waits=[mwr, wbt_free], start=True,
                               stop=True)
                    mwbe = P.op("ve", "tensor_copy", wbt, psb[6][:, :], waits=[mwb])
                    wbt_free = mwbe
                    s5_free = mwr
                    for pp in range(3):
                        if mq2[pp] is not None:
                            bband_free[pp] = mq2[pp]
                    if debug and j == 0 and h == 0:
                        mdn1 = P.op("ve", "tensor_copy", P_(19200, 4), s4p, waits=[ms4p])
                        mdn2 = P.op("ve", "tensor_copy", P_(19204, 4), mad4, waits=[mmad, mdn1])
                        mdn3 = P.op("ve", "tensor_copy", P_(19208, 4), wch1, waits=[mw4, mdn2])
                        mdn4 = P.op("ve", "tensor_copy", P_(19212, 4), w4, waits=[mw4v, mdn3])
                        P.op("gp", "dma_start", sem="c_gdbg", amt=16,
                             out=dbg["d_nrm"][:, 0:8],
                             in_=P_(19200, 8), waits=[mdn1, mdn2, mdn3, mdn4])
                    myt = P.op("ve", "tensor_tensor", yt[64:128, :], psb[4][64:128, :],
                               wbt[64:128, :], op=AOP.mult,
                               waits=[mwbe, yt_free] + ([P.mark("c_gdbg")] if debug else []))
                    s4_free = myt
                    yt_free = P.op("gp", "dma_start", sem="c_gyt", amt=16,
                                   out=yT_s[h * 64:(h + 1) * 64, j * 512:(j + 1) * 512],
                                   in_=yt[64:128, :], waits=[myt])
                v_free[hp] = mav
                if h + 2 < NH:
                    mvl_marks[h + 2] = emit_vload(h + 2)

            if debug:
                P.barrier()
                for r in range(8):
                    mld = P.op("sy", "dma_start", sem="c_dbg", amt=16, out=P_(0, 2048),
                               in_=yT_s[r * 128:(r + 1) * 128, :], waits=[P.mark("c_gdbg")])
                    P.op("gp", "dma_start", sem="c_gdbg", amt=16,
                         out=dbg["d_yT"][r * 128:(r + 1) * 128, :], in_=P_(0, 2048),
                         waits=[mld])

            P.barrier()

            bpb = P_(14336, 1024)
            mbp = P.op("sy", "dma_start", sem="c_ini", amt=16, out=bpb[:1, :],
                       in_=bp.unsqueeze(0))
            for hh in range(2):
                mb = P.op("te", "matmul", psb[6][:, :], ones[:1, 0:128],
                          bpb[:1, hh * 512:(hh + 1) * 512],
                          waits=[mbp, P.mark("ve")], start=True, stop=True)
                P.op("ve", "tensor_copy", bpb[:, hh * 512:(hh + 1) * 512], psb[6][:, :],
                     waits=[mb])
            m_bpb = P.mark("ve")
            mwp = []
            for cc in range(8):
                mwp.append(P.op("sy", "dma_start", sem="c_wp", amt=16,
                                out=P_(cc * 1024, 1024),
                                in_=Wp[cc * 128:(cc + 1) * 128, :],
                                waits=[P.mark("te"), P.mark("ve")]))
            yst_free = [P.mark("te"), P.mark("te")]
            stage_free = [P.mark("c_gxp0"), P.mark("c_gxp1")]
            prev_evac = P.mark("ve")
            for pass4 in range(4):
                mxr = []
                for tt in range(4):
                    mxr.append(P.op("sy", "dma_start", sem="c_xr", amt=16,
                                    out=P_(9216 + tt * 1024, 1024),
                                    in_=xq_in[pass4 * 512 + tt * 128:
                                              pass4 * 512 + (tt + 1) * 128, :],
                                    waits=[prev_evac]))
                mmm = [None] * 8
                for cc in range(8):
                    pv = cc % 2
                    yslot = P_(8192 + pv * 512, 512)
                    myl = P.op("sy", "dma_start", sem="c_y%d" % pv, amt=16,
                               out=yslot,
                               in_=yT_s[cc * 128:(cc + 1) * 128,
                                        pass4 * 512:(pass4 + 1) * 512],
                               waits=[yst_free[pv]])
                    mlast = None
                    for tt in range(4):
                        for oh in range(2):
                            pbi = tt * 2 + oh
                            w = [myl, mwp[7]]
                            if mmm[pbi] is not None:
                                w.append(mmm[pbi])
                            elif pass4 > 0:
                                w.append(prev_evac)
                            mlast = P.op("te", "matmul", psb[pbi][:, :],
                                         yslot[:, tt * 128:(tt + 1) * 128],
                                         P_(cc * 1024, 1024)[:, oh * 512:(oh + 1) * 512],
                                         waits=w, start=(cc == 0), stop=(cc == 7))
                            mmm[pbi] = mlast
                    yst_free[pv] = mlast
                for tt in range(4):
                    for oh in range(2):
                        pbi = tt * 2 + oh
                        pv = pbi % 2
                        stg = P_(13312 + pv * 512, 512)
                        m1 = P.op("ve", "tensor_tensor", stg, psb[pbi][:, :],
                                  bpb[:, oh * 512:(oh + 1) * 512], op=AOP.add,
                                  waits=[mmm[pbi], m_bpb, stage_free[pv]])
                        m2 = P.op("ve", "tensor_tensor", stg, stg,
                                  P_(9216 + tt * 1024, 1024)[:, oh * 512:(oh + 1) * 512],
                                  op=AOP.add, waits=[m1, mxr[3]])
                        stage_free[pv] = P.op("gp", "dma_start", sem="c_gxp%d" % pv, amt=16,
                                              out=xp_s[pass4 * 512 + tt * 128:
                                                       pass4 * 512 + (tt + 1) * 128,
                                                       oh * 512:(oh + 1) * 512],
                                              in_=stg, waits=[m2])
                prev_evac = P.mark("ve")

            if debug:
                P.barrier()
                for r in range(16):
                    mld = P.op("sy", "dma_start", sem="c_dbg", amt=16, out=P_(0, 1024),
                               in_=xp_s[r * 128:(r + 1) * 128, :], waits=[P.mark("c_gdbg")])
                    P.op("gp", "dma_start", sem="c_gdbg", amt=16,
                         out=dbg["d_xp"][r * 128:(r + 1) * 128, :], in_=P_(0, 1024),
                         waits=[mld])

            P.barrier()

            m2c = P_(5120, 16)
            r2c = P_(5136, 16)
            rsc = P_(5152, 16)
            gacc = P_(5220, 1)
            mgacc = P.op("ve", "memset", gacc, 0.0)
            xt = P_(0, 1024)
            xt_free = P.mark("ve")
            for ti in range(16):
                mld = P.op("sy", "dma_start", sem="c_x", amt=16, out=xt,
                           in_=xp_s[ti * 128:(ti + 1) * 128, :], waits=[xt_free])
                sq = P_(1024, 1024)
                mS = P.op("ve", "tensor_reduce", m2c[:, ti:ti + 1], xt, axis=AX.X,
                          op=AOP.add, waits=[mld])
                mM = P.op("ve", "tensor_scalar", m2c[:, ti:ti + 1], m2c[:, ti:ti + 1],
                          1.0 / C, None, op0=AOP.mult, waits=[mS])
                mQ = P.op("ve", "tensor_tensor", sq, xt, xt, op=AOP.mult, waits=[mld])
                mS2 = P.op("ve", "tensor_reduce", smal[:, 12:13], sq, axis=AX.X,
                           op=AOP.add, waits=[mQ])
                mE2 = P.op("ve", "tensor_scalar", smal[:, 12:13], smal[:, 12:13],
                           1.0 / C, None, op0=AOP.mult, waits=[mS2])
                mMM = P.op("ve", "tensor_tensor", smal[:, 13:14], m2c[:, ti:ti + 1],
                           m2c[:, ti:ti + 1], op=AOP.mult, waits=[mM])
                mV = P.op("ve", "tensor_tensor", smal[:, 12:13], smal[:, 12:13],
                          smal[:, 13:14], op=AOP.subtract, waits=[mE2, mMM])
                mVe = P.op("ve", "tensor_scalar", smal[:, 12:13], smal[:, 12:13],
                           1e-5, None, op0=AOP.add, waits=[mV])
                mR = P.op("ve", "reciprocal", smal[:, 12:13], smal[:, 12:13], waits=[mVe])
                mRs = P.op("sc", "activation", out=r2c[:, ti:ti + 1], in_=smal[:, 12:13],
                           func=AF.Sqrt, waits=[mR])
                mdx = P.op("ve", "tensor_scalar", sq, xt, m2c[:, ti:ti + 1], None,
                           op0=AOP.subtract, accum_out=smal[:, 14:15], op1=AOP.max,
                           waits=[mM])
                mdn = P.op("ve", "tensor_scalar", sq, xt, m2c[:, ti:ti + 1], None,
                           op0=AOP.subtract, accum_out=smal[:, 15:16], op1=AOP.min,
                           waits=[mdx])
                mneg = P.op("ve", "tensor_scalar", smal[:, 15:16], smal[:, 15:16],
                            -1.0, None, op0=AOP.mult, waits=[mdn])
                mmx = P.op("ve", "tensor_tensor", smal[:, 14:15], smal[:, 14:15],
                           smal[:, 15:16], op=AOP.max, waits=[mneg])
                mdev = P.op("ve", "tensor_tensor", smal[:, 14:15], smal[:, 14:15],
                            r2c[:, ti:ti + 1], op=AOP.mult, waits=[mmx, mRs])
                mgacc = P.op("ve", "tensor_tensor", gacc, gacc, smal[:, 14:15],
                             op=AOP.max, waits=[mdev, mgacc])
                xt_free = mdn
            mcc_in = P.op("gp", "dma_start", sem="c_gcc", amt=16, out=cc1_in[:],
                          in_=gacc.squeeze(1), waits=[mgacc])
            mcc = P.op("gp", "collective_compute", "AllReduce", AOP.max,
                       sem="cc", amt=1, replica_groups=[list(range(8))],
                       ins=[cc1_in[:]], outs=[cc1_out[:]], waits=[mcc_in])
            mccb = P.op("gp", "dma_start", sem="c_gcc", amt=16, out=P_(5221, 1),
                        in_=cc1_out[:].unsqueeze(1), waits=[mcc])
            mgr = P.op("gp", "tensor_reduce", P_(5222, 1)[:1, :], P_(5221, 1),
                       sem="gpc", axis=AX.C, op=AOP.max, waits=[mccb])
            msig = P.op("sc", "activation", out=smal[:1, 5:6], in_=smal[:1, 1:2],
                        func=AF.Sigmoid, waits=[m_ld0])
            msx = P.op("ve", "tensor_tensor", smal[:1, 5:6], smal[:1, 5:6],
                       P_(5222, 1)[:1, :], op=AOP.mult, waits=[msig, mgr])
            msx2 = P.op("ve", "tensor_scalar", smal[:1, 5:6], smal[:1, 5:6],
                        1.0 / QMAX, None, op0=AOP.mult, waits=[msx])
            mix = P.op("ve", "reciprocal", smal[:1, 6:7], smal[:1, 5:6], waits=[msx2])
            mbx = bcast_scalar(smal[:1, 5:6], sxv[:, 1:2], [msx2])
            mbix = bcast_scalar(smal[:1, 6:7], sxv[:, 0:1], [mix])
            mrs = P.op("ve", "tensor_scalar", rsc, r2c, sxv[:, 0:1], None,
                       op0=AOP.mult, waits=[mbix])

            xt_free = P.mark("ve")
            xq_ev_free = P.mark("ve")
            acc_tr_free = P.mark("te")
            for ti in range(16):
                mld = P.op("sy", "dma_start", sem="c_x", amt=16, out=xt,
                           in_=xp_s[ti * 128:(ti + 1) * 128, :], waits=[xt_free])
                r = P_(2048, 1024)
                acc = P_(3072, 1024)
                bt = P_(4096, 1024)
                m1 = P.op("ve", "tensor_scalar", r, xt, m2c[:, ti:ti + 1],
                          rsc[:, ti:ti + 1], op0=AOP.subtract, op1=AOP.mult,
                          waits=[mld, mrs])
                m2_ = P.op("ve", "tensor_scalar", r, r, 127.49, -128.49,
                           op0=AOP.min, op1=AOP.max, waits=[m1])
                m3 = P.op("ve", "tensor_scalar", r, r, 129.5, None,
                          op0=AOP.add, waits=[m2_])
                macc = P.op("ve", "memset", acc, -129.0, waits=[acc_tr_free])
                prev = m3
                for k in range(8, -1, -1):
                    th = float(2 ** k)
                    mb_ = P.op("ve", "tensor_scalar", bt, r, th, -th,
                               op0=AOP.is_ge, op1=AOP.mult, waits=[prev])
                    P.op("ve", "tensor_tensor", r, r, bt, op=AOP.add, waits=[mb_])
                    prev = P.op("ve", "tensor_tensor", acc, acc, bt, op=AOP.subtract)
                xt_free = m1
                last_ev = xq_ev_free
                for hc4 in range(0, 8, 4):
                    mts = None
                    for jj in range(4):
                        hc = hc4 + jj
                        mts = P.op("te", "transpose",
                                   psb[4][:, jj * 128:(jj + 1) * 128],
                                   acc[:, hc * 128:(hc + 1) * 128], ident[:],
                                   waits=[prev, macc, last_ev])
                    mcps = []
                    for jj in range(4):
                        hc = hc4 + jj
                        mcps.append(P.op("ve", "tensor_copy",
                                         xqT[:, hc * TQ + ti * 128:hc * TQ + (ti + 1) * 128],
                                         psb[4][:, jj * 128:(jj + 1) * 128],
                                         waits=[mts]))
                    last_ev = mcps[-1]
                    acc_tr_free = mts
                xq_ev_free = last_ev

            if debug:
                P.barrier()
                for hc in range(8):
                    mcv = P.op("ve", "tensor_copy", P_(6144, 2048),
                               xqT[:, hc * TQ:(hc + 1) * TQ], waits=[P.mark("c_gdbg")])
                    P.op("gp", "dma_start", sem="c_gdbg", amt=16,
                         out=dbg["d_xqi"][hc * 128:(hc + 1) * 128, :], in_=P_(6144, 2048),
                         waits=[mcv])
                mss = P.mark("ve")
                P.op("gp", "dma_start", sem="c_gdbg", amt=16, out=dbg["d_scal"][:, 0:2],
                     in_=sxv[:, 0:2], waits=[mss])

            P.barrier()

            msg = P.op("ve", "tensor_scalar", P_(4096, NIC), wsg_t[:, 0:NIC],
                       sxv[:, 1:2], None, op0=AOP.mult, waits=[mbx])
            msu = P.op("ve", "tensor_scalar", P_(4160, NIC), wsu_t[:, 0:NIC],
                       sxv[:, 1:2], None, op0=AOP.mult, waits=[msg])
            sgv_t = P_(4096, NIC)
            suv_t = P_(4160, NIC)
            mhab = P.op("ve", "memset", habs[:], 0.0)
            wq_free = [P.mark("te"), P.mark("te")]
            hp_free = [P.mark("c_ghp0"), P.mark("c_ghp1")]
            gs_free = P.mark("ve")
            for ic in range(NIC):
                pv = ic % 2
                wgsl = s_wgu[:, pv * 2048:pv * 2048 + 1024]
                wusl = s_wgu[:, pv * 2048 + 1024:pv * 2048 + 2048]
                mlg = P.op("sy", "dma_start", sem="c_w%d" % pv, amt=16,
                           out=wgsl.rearrange("p (a c) -> p a c", c=128),
                           in_=wgq_s[:, ic * 128:(ic + 1) * 128].rearrange(
                               "(a p) c -> p a c", p=128),
                           waits=[wq_free[pv]])
                mlu = P.op("sy", "dma_start", sem="c_w%d" % pv, amt=16,
                           out=wusl.rearrange("p (a c) -> p a c", c=128),
                           in_=wuq_s[:, ic * 128:(ic + 1) * 128].rearrange(
                               "(a p) c -> p a c", p=128))
                mlast = None
                for tb in range(4):
                    pbg = psb[0 + (tb % 2) * 2]
                    pbu = psb[1 + (tb % 2) * 2]
                    mg = None
                    mu = None
                    for fc in range(8):
                        wg_w = [mlu]
                        if fc == 0:
                            wg_w.append(gs_free if tb < 2 and ic == 0 else P.mark("sc"))
                        mg = P.op("te", "matmul", pbg[:, :],
                                  wgsl[:, fc * 128:(fc + 1) * 128],
                                  xqT[:, fc * TQ + tb * 512:fc * TQ + (tb + 1) * 512],
                                  waits=wg_w, start=(fc == 0), stop=(fc == 7),
                                  nopc=BF16_NOP)
                        mu = P.op("te", "matmul", pbu[:, :],
                                  wusl[:, fc * 128:(fc + 1) * 128],
                                  xqT[:, fc * TQ + tb * 512:fc * TQ + (tb + 1) * 512],
                                  waits=[mlu] + ([P.mark("ve")] if fc == 0 else []),
                                  start=(fc == 0), stop=(fc == 7), nopc=BF16_NOP)
                    gs = P_(2048, 512)
                    mgs = P.op("sc", "activation", out=gs, in_=pbg[:, :], func=AF.Silu,
                               scale=sgv_t[:, ic:ic + 1], waits=[mg, msg, gs_free])
                    hpt = P_(2560 + (tb % 2) * 512, 512)
                    mhp = P.op("ve", "scalar_tensor_tensor", hpt, pbu[:, :],
                               suv_t[:, ic:ic + 1], gs, op0=AOP.mult, op1=AOP.mult,
                               waits=[mu, mgs, msu, hp_free[tb % 2]])
                    gs_free = mhp
                    mlast = mhp
                    mrm = P.op("ve", "tensor_reduce", smal[:, 12:13], hpt, axis=AX.X,
                               op=AOP.max, apply_absolute_value=True, waits=[mhp])
                    mhab = P.op("ve", "tensor_tensor", habs[:], habs[:], smal[:, 12:13],
                                op=AOP.max, waits=[mrm, mhab])
                    hp_free[tb % 2] = P.op("gp", "dma_start", sem="c_ghp%d" % (tb % 2),
                                           amt=16,
                                           out=hp_s[ic * 128:(ic + 1) * 128,
                                                    tb * 512:(tb + 1) * 512],
                                           in_=hpt, waits=[mhp])
                wq_free[pv] = mlast

            if debug:
                P.barrier()
                for r in range(NIC):
                    mld = P.op("sy", "dma_start", sem="c_dbg", amt=16, out=P_(6144, 2048),
                               in_=hp_s[r * 128:(r + 1) * 128, :], waits=[P.mark("c_gdbg")])
                    P.op("gp", "dma_start", sem="c_gdbg", amt=16,
                         out=dbg["d_hp"][r * 128:(r + 1) * 128, :], in_=P_(6144, 2048),
                         waits=[mld])

            mcc_in2 = P.op("gp", "dma_start", sem="c_gcc", amt=16, out=cc2_in[:],
                           in_=habs[:].squeeze(1), waits=[mhab])
            mcc2 = P.op("gp", "collective_compute", "AllReduce", AOP.max,
                        sem="cc", amt=1, replica_groups=[list(range(8))],
                        ins=[cc2_in[:]], outs=[cc2_out[:]], waits=[mcc_in2])
            mccb2 = P.op("gp", "dma_start", sem="c_gcc", amt=16, out=P_(5221, 1),
                         in_=cc2_out[:].unsqueeze(1), waits=[mcc2])
            mgr2 = P.op("gp", "tensor_reduce", P_(5222, 1)[:1, :], P_(5221, 1),
                        sem="gpc", axis=AX.C, op=AOP.max, waits=[mccb2])
            msg2 = P.op("sc", "activation", out=smal[:1, 7:8], in_=smal[:1, 2:3],
                        func=AF.Sigmoid, waits=[m_ld0])
            msh = P.op("ve", "tensor_tensor", smal[:1, 7:8], smal[:1, 7:8],
                       P_(5222, 1)[:1, :], op=AOP.mult, waits=[msg2, mgr2])
            msh2 = P.op("ve", "tensor_scalar", smal[:1, 7:8], smal[:1, 7:8],
                        1.0 / QMAX, None, op0=AOP.mult, waits=[msh])
            mih = P.op("ve", "reciprocal", smal[:1, 8:9], smal[:1, 7:8], waits=[msh2])
            mbh = bcast_scalar(smal[:1, 7:8], sxv[:, 3:4], [msh2])
            mbih = bcast_scalar(smal[:1, 8:9], sxv[:, 2:3], [mih])
            mwf = P.op("ve", "tensor_scalar", wdsb[:, :], wdsb[:, :], sxv[:, 3:4],
                       None, op0=AOP.mult, waits=[mbh])

            P.barrier()

            hq_free = [P.mark("te"), P.mark("te")]
            wd_free = [P.mark("te"), P.mark("te")]
            st_free = [P.mark("c_go0"), P.mark("c_go1")]
            lad_free = P.mark("ve")
            for tb in range(4):
                mdown = [None] * 8
                for ic in range(NIC):
                    pv = ic % 2
                    hsl = P_(pv * 512, 512)
                    mlh = P.op("sy", "dma_start", sem="c_h%d" % pv, amt=16, out=hsl,
                               in_=hp_s[ic * 128:(ic + 1) * 128, tb * 512:(tb + 1) * 512],
                               waits=[lad_free])
                    wdsl = s_wd2[:, pv * 1024:(pv + 1) * 1024]
                    mlw = P.op("sy", "dma_start", sem="c_wd%d" % pv, amt=16, out=wdsl,
                               in_=wdq_s[ic * 128:(ic + 1) * 128, :],
                               waits=[wd_free[pv]])
                    r = P_(1024, 512)
                    acc = P_(1536, 512)
                    bt = P_(2048, 512)
                    m1 = P.op("ve", "tensor_scalar", r, hsl, sxv[:, 2:3], 127.49,
                              op0=AOP.mult, op1=AOP.min, waits=[mlh, mbih])
                    m2_ = P.op("ve", "tensor_scalar", r, r, -128.49, 129.5,
                               op0=AOP.max, op1=AOP.add, waits=[m1])
                    macc = P.op("ve", "memset", acc, -129.0)
                    prev = m2_
                    for k in range(8, -1, -1):
                        th = float(2 ** k)
                        mb_ = P.op("ve", "tensor_scalar", bt, r, th, -th,
                                   op0=AOP.is_ge, op1=AOP.mult, waits=[prev])
                        P.op("ve", "tensor_tensor", r, r, bt, op=AOP.add, waits=[mb_])
                        prev = P.op("ve", "tensor_tensor", acc, acc, bt, op=AOP.subtract)
                    lad_free = m1
                    hqsl = s_hq[:, pv * 512:(pv + 1) * 512]
                    mhq = P.op("ve", "tensor_copy", hqsl, acc,
                               waits=[prev, macc, hq_free[pv]])
                    if debug and tb == 0 and ic == 0:
                        mcvh = P.op("ve", "tensor_copy", P_(8192, 512), hqsl,
                                    waits=[mhq])
                        P.op("gp", "dma_start", sem="c_gdbg", amt=16,
                             out=dbg["d_hq"], in_=P_(8192, 512), waits=[mcvh])
                        mcvs = P.op("ve", "tensor_copy", P_(8704, 4), sxv[:, 0:4],
                                    waits=[mcvh])
                        P.op("gp", "dma_start", sem="c_gdbg", amt=16,
                             out=dbg["d_scal"][:, 2:6], in_=P_(8704, 4), waits=[mcvs])
                    mlast = None
                    for tt in range(4):
                        for oh in range(2):
                            pbi = tt * 2 + oh
                            w = [mhq, mlw]
                            if mdown[pbi] is not None:
                                w.append(mdown[pbi])
                            elif tb > 0:
                                w.append(P.mark("ve"))
                            mlast = P.op("te", "matmul", psb[pbi][:, :],
                                         hqsl[:, tt * 128:(tt + 1) * 128],
                                         wdsl[:, oh * 512:(oh + 1) * 512],
                                         waits=w, start=(ic == 0), stop=(ic == NIC - 1),
                                         nopc=BF16_NOP)
                            mdown[pbi] = mlast
                    hq_free[pv] = mlast
                    wd_free[pv] = mlast
                mxr_free = P.mark("ve")
                for tt in range(4):
                    mxr = P.op("sy", "dma_start", sem="c_xr%s" % ("" if tt % 2 == 0 else "2"),
                               amt=16,
                               out=P_(4096 + (tt % 2) * 1024, 1024),
                               in_=xp_s[tb * 512 + tt * 128:tb * 512 + (tt + 1) * 128, :],
                               waits=[mxr_free if tt < 2 else P.mark("ve")])
                    for oh in range(2):
                        pbi = tt * 2 + oh
                        pv = pbi % 2
                        stg = P_(6144 + pv * 512, 512)
                        m1 = P.op("ve", "tensor_tensor", stg, psb[pbi][:, :],
                                  wdsb[:, oh * 512:(oh + 1) * 512], op=AOP.mult,
                                  waits=[mdown[pbi], mwf, st_free[pv]])
                        m2_ = P.op("ve", "tensor_tensor", stg, stg,
                                   P_(4096 + (tt % 2) * 1024, 1024)[:, oh * 512:(oh + 1) * 512],
                                   op=AOP.add, waits=[m1, mxr])
                        st_free[pv] = P.op("gp", "dma_start", sem="c_go%d" % pv, amt=16,
                                           out=out[tb * 512 + tt * 128:
                                                   tb * 512 + (tt + 1) * 128,
                                                   oh * 512:(oh + 1) * 512],
                                           in_=stg, waits=[m2_])

            P.wait("gp", [P.mark("c_go0"), P.mark("c_go1")])

        handles_names = ["te", "sc", "ve", "gp", "sy"]

        def run_engine(name, h):
            hh = dict.fromkeys(handles_names, None)
            hh[name] = h
            schedule(Prog(hh, sems))

        @block.sync
        def _(sync):
            run_engine("sy", sync)

        @block.tensor
        def _(tensor):
            run_engine("te", tensor)

        @block.scalar
        def _(scalar):
            run_engine("sc", scalar)

        @block.vector
        def _(vector):
            run_engine("ve", vector)

        @block.gpsimd
        def _(gpsimd):
            run_engine("gp", gpsimd)

    return nc, dbg


def make_masks(p):
    import ml_dtypes
    m = np.zeros((4, 8, 128, 512), dtype=np.float32)
    stripes = STRIPES[p]
    for j in range(4):
        s = stripes[j]
        q0 = s * 512
        for w in range(8):
            kc = KCDM[j] - 8 + w
            kk = kc * 128 + np.arange(128)[:, None]
            qq = q0 + np.arange(512)[None, :]
            m[j, w] = (kk <= qq).astype(np.float32)
    return m.astype(ml_dtypes.bfloat16)


def kernel(L0_attn_x, L0_attn_W_attn, L0_attn_b_attn, L0_attn_W_proj,
           L0_attn_b_proj, L0_attn_score_gain, L0_ffn_clip_x, L0_ffn_clip_down,
           L0_ffn_w_gate, L0_ffn_w_up, L0_ffn_w_down):
    x = np.ascontiguousarray(np.asarray(L0_attn_x, dtype=np.float32))
    Wa_ = np.ascontiguousarray(np.asarray(L0_attn_W_attn, dtype=np.float32))
    ba_ = np.ascontiguousarray(np.asarray(L0_attn_b_attn, dtype=np.float32))
    Wp_ = np.ascontiguousarray(np.asarray(L0_attn_W_proj, dtype=np.float32))
    bp_ = np.ascontiguousarray(np.asarray(L0_attn_b_proj, dtype=np.float32))
    g_ = np.asarray(L0_attn_score_gain, dtype=np.float32).reshape(1)
    cx_ = np.asarray(L0_ffn_clip_x, dtype=np.float32).reshape(1)
    cd_ = np.asarray(L0_ffn_clip_down, dtype=np.float32).reshape(1)
    wg_ = np.ascontiguousarray(np.asarray(L0_ffn_w_gate, dtype=np.float32))
    wu_ = np.ascontiguousarray(np.asarray(L0_ffn_w_up, dtype=np.float32))
    wd_ = np.ascontiguousarray(np.asarray(L0_ffn_w_down, dtype=np.float32))
    ident = np.eye(128, dtype=np.float32)

    nc, _ = build_program(debug=False)
    in_maps = []
    for c in range(8):
        b, p = c // 2, c % 2
        rows = np.concatenate([np.arange(s * 512, (s + 1) * 512) for s in STRIPES[p]])
        xb = np.ascontiguousarray(x[b])
        in_maps.append({
            "x": xb, "xq": np.ascontiguousarray(xb[rows]),
            "Wa": Wa_, "ba": ba_, "Wp": Wp_, "bp": bp_,
            "gsc": g_, "clipx": cx_, "clipd": cd_,
            "wg": wg_, "wu": wu_, "wd": wd_,
            "ident": ident, "m01": make_masks(p),
        })
    res = run_bass_kernel_spmd(nc, in_maps, core_ids=list(range(8)))
    outp = np.empty((B, T, C), dtype=np.float32)
    for c in range(8):
        b, p = c // 2, c % 2
        o = res.results[c]["out"]
        for jj, s in enumerate(STRIPES[p]):
            outp[b, s * 512:(s + 1) * 512, :] = o[jj * 512:(jj + 1) * 512, :]
    return outp


# version 08
# speedup vs baseline: 1.0058x; 1.0026x over previous
import os
import numpy as np
import concourse.bass as bass
import concourse.mybir as mybir
from concourse.bass_utils import run_bass_kernel_spmd
from contextlib import ExitStack

_SIM = os.environ.get("BASS_SIM_FP64") == "1"

F32 = mybir.dt.float32
BF16 = mybir.dt.bfloat16
AOP = mybir.AluOpType
AF = mybir.ActivationFunctionType
AX = mybir.AxisListType

B, T, C = 4, 4096, 1024
NH, D = 16, 64
C3 = 3 * C
I = 2816
TQ = 2048
NTB = 8
NIC = I // 128
QMAX = 127.0
KCDM = [8, 16, 24, 32]
STRIPES = [[0, 3, 4, 7], [1, 2, 5, 6]]

BF16_NOP = 280
K64_NOP = 130
TR_NOP = 280


class Prog:
    def __init__(self, handles, sems):
        self.h = handles
        self.sems = sems
        self.cnt = {k: 0 for k in sems}
        self.lastwait = {}

    def wait(self, eng, marks):
        h = self.h[eng]
        for (s, v) in marks:
            if v <= 0:
                continue
            key = (eng, s)
            if self.lastwait.get(key, -1) >= v:
                continue
            self.lastwait[key] = v
            if h is not None:
                h.wait_ge(self.sems[s], v)

    def op(self, eng, meth, *a, waits=(), sem=None, amt=1, nopc=0, **kw):
        self.wait(eng, waits)
        h = self.h[eng]
        if sem is None:
            sem = eng
        self.cnt[sem] += amt
        if h is not None:
            ins = getattr(h, meth)(*a, **kw)
            ins.then_inc(self.sems[sem], amt)
            if nopc and not _SIM:
                h.nop(cycle_cnt=nopc)
        return (sem, self.cnt[sem])

    def mark(self, sem):
        return (sem, self.cnt[sem])

    def barrier(self):
        marks = [self.mark(s) for s in self.cnt]
        for eng in ("te", "sc", "ve", "gp", "sy"):
            self.wait(eng, marks)


def build_program(debug=False):
    nc = bass.Bass("TRN2", target_bir_lowering=False)

    x_in = nc.dram_tensor("x", [T, C], F32, kind="ExternalInput").ap()
    xq_in = nc.dram_tensor("xq", [TQ, C], F32, kind="ExternalInput").ap()
    Wa = nc.dram_tensor("Wa", [C, C3], F32, kind="ExternalInput").ap()
    ba = nc.dram_tensor("ba", [C3], F32, kind="ExternalInput").ap()
    Wp = nc.dram_tensor("Wp", [C, C], F32, kind="ExternalInput").ap()
    bp = nc.dram_tensor("bp", [C], F32, kind="ExternalInput").ap()
    gsc = nc.dram_tensor("gsc", [1], F32, kind="ExternalInput").ap()
    clipx = nc.dram_tensor("clipx", [1], F32, kind="ExternalInput").ap()
    clipd = nc.dram_tensor("clipd", [1], F32, kind="ExternalInput").ap()
    wg = nc.dram_tensor("wg", [I, C], F32, kind="ExternalInput").ap()
    wu = nc.dram_tensor("wu", [I, C], F32, kind="ExternalInput").ap()
    wd = nc.dram_tensor("wd", [C, I], F32, kind="ExternalInput").ap()
    ident_in = nc.dram_tensor("ident", [128, 128], F32, kind="ExternalInput").ap()
    m01_in = nc.dram_tensor("m01", [4, 8, 128, 512], BF16, kind="ExternalInput").ap()

    out = nc.dram_tensor("out", [TQ, C], F32, kind="ExternalOutput").ap()

    qT_s = nc.dram_tensor("qT_s", [C, TQ], F32).ap()
    kT_s = nc.dram_tensor("kT_s", [C, T], F32).ap()
    vT_s = nc.dram_tensor("vT_s", [T, C], F32).ap()
    yT_s = nc.dram_tensor("yT_s", [C, TQ], F32).ap()
    xp_s = nc.dram_tensor("xp_s", [TQ, C], F32).ap()
    hp_s = nc.dram_tensor("hp_s", [I, TQ], F32).ap()
    km_s = nc.dram_tensor("km_s", [128, 8], F32).ap()
    wgq_s = nc.dram_tensor("wgq_s", [C, I], BF16).ap()
    wuq_s = nc.dram_tensor("wuq_s", [C, I], BF16).ap()
    wdq_s = nc.dram_tensor("wdq_s", [I, C], BF16).ap()
    cc1_in = nc.dram_tensor("cc1_in", [128], F32).ap()
    cc1_out = nc.dram_tensor("cc1_out", [128], F32, addr_space="Shared").ap()
    cc2_in = nc.dram_tensor("cc2_in", [128], F32).ap()
    cc2_out = nc.dram_tensor("cc2_out", [128], F32, addr_space="Shared").ap()

    dbg = {}
    if debug:
        for nm, shp in [("d_qT", [C, TQ]), ("d_kT", [C, T]), ("d_vT", [T, C]),
                        ("d_yT", [C, TQ]), ("d_xp", [TQ, C]), ("d_hp", [I, TQ]),
                        ("d_scal", [128, 16]), ("d_xqi", [C, TQ]),
                        ("d_e", [128, 512]), ("d_q2", [128, 512]),
                        ("d_u4", [128, 512]), ("d_nrm", [128, 8]),
                        ("d_wdq", [I, C]), ("d_hq", [128, 512])]:
            dbg[nm] = nc.dram_tensor(nm, shp, F32, kind="ExternalOutput").ap()

    with ExitStack() as ctx:
        sbt = lambda name, shape, dt=F32: ctx.enter_context(nc.sbuf_tensor(name, shape, dt))

        ident = sbt("s_ident", [128, 128])
        identb = sbt("s_identb", [128, 128], BF16)
        ones = sbt("s_ones", [128, 130])
        ba_t = sbt("s_ba", [128, 24])
        wsg_t = sbt("s_wsg", [128, NIC])
        wsu_t = sbt("s_wsu", [128, NIC])
        wsd_t = sbt("s_wsd", [128, 8])
        wdsb = sbt("s_wdsb", [128, C])
        kmean = sbt("s_kmean", [128, 8])
        ksump = sbt("s_ksump", [128, 64])
        smal = sbt("s_smal", [128, 48])
        invg = sbt("s_invg", [128, 1])
        habs = sbt("s_habs", [128, 1])
        sxv = sbt("s_sxv", [128, 4])
        xqT = sbt("s_xqT", [128, 8 * TQ], BF16)
        mtiles = sbt("s_mt", [128, 32 * 512], BF16)
        pool = sbt("s_pool", [128, 24576], F32)
        s_wstage = sbt("s_wstage", [128, 512], BF16)
        s_wgu = sbt("s_wgu", [128, 4096], BF16)
        s_wd2 = sbt("s_wd2", [128, 2048], BF16)
        s_hq = sbt("s_hq", [128, 1024], BF16)
        psb = [ctx.enter_context(nc.psum_tensor(f"pb{i}", [128, 512], F32)) for i in range(8)]

        sem_names = ["te", "sc", "ve", "gpc", "cc", "dout",
                     "c_ini", "c_w0", "c_w1", "c_x", "c_wa0", "c_wa1",
                     "c_vw0", "c_vw1", "c_dbg", "c_msk", "c_km", "c_k",
                     "c_q0", "c_q1", "c_v0", "c_v1", "c_wp", "c_xr",
                     "c_y0", "c_y1", "c_h0", "c_h1", "c_wd0", "c_wd1",
                     "c_gst", "c_gkq0", "c_gkq1", "c_gv0", "c_gv1",
                     "c_gkm", "c_gyt", "c_gxp0", "c_gxp1", "c_ghp0",
                     "c_ghp1", "c_gcc", "c_gdbg", "c_go0", "c_go1", "c_xr2"]
        sems = {n: ctx.enter_context(nc.semaphore("sem_" + n)) for n in sem_names}
        block = ctx.enter_context(nc.Block())

        def P_(a, n):
            assert a + n <= 24576, (a, n)
            return pool[:, a:a + n]

        def schedule(P: Prog):
            P.op("sy", "dma_start", sem="c_ini", amt=16, out=ident[:], in_=ident_in)
            P.op("sy", "dma_start", sem="c_ini", amt=16, out=ba_t[:, 0:24],
                 in_=ba.rearrange("(c p) -> p c", p=128), allow_slow_non_contiguous=True)
            P.op("sy", "dma_start", sem="c_ini", amt=16, out=smal[:1, 0:1], in_=gsc.unsqueeze(0))
            P.op("sy", "dma_start", sem="c_ini", amt=16, out=smal[:1, 1:2], in_=clipx.unsqueeze(0))
            P.op("sy", "dma_start", sem="c_ini", amt=16, out=smal[:1, 2:3], in_=clipd.unsqueeze(0))
            m_ld0 = P.mark("c_ini")
            m_ones = P.op("ve", "memset", ones[:], 1.0)
            m_idb = P.op("ve", "tensor_copy", identb[:], ident[:], waits=[m_ld0])
            P.op("ve", "memset", habs[:], 0.0)
            P.op("ve", "memset", pool[:, 0:512], 0.0)

            m_warm = P.op("te", "matmul", psb[7][:1, :], ones[:, 0:1], pool[:, 0:512],
                          waits=[m_ones, P.mark("ve")], start=True, stop=True)

            m_rg = P.op("ve", "reciprocal", smal[:1, 4:5], smal[:1, 0:1], waits=[m_ld0])
            m_bg = P.op("te", "matmul", psb[6][:, 0:1], ones[:1, 0:128], smal[:1, 4:5],
                        waits=[m_rg, m_warm], start=True, stop=True)
            m_invg = P.op("ve", "tensor_copy", invg[:], psb[6][:, 0:1], waits=[m_bg])

            def bcast_scalar(src_ap, dst_ap, waits):
                mm = P.op("te", "matmul", psb[6][:, 4:5], ones[:1, 0:128], src_ap,
                          waits=list(waits) + [P.mark("ve")], start=True, stop=True)
                return P.op("ve", "tensor_copy", dst_ap, psb[6][:, 4:5], waits=[mm])

            def ladder_w(src_ap, n, invs_ap, acc_waits, waits):
                r = P_(4096, n)
                acc = P_(6144, n)
                bt = P_(8192, n)
                m1 = P.op("ve", "tensor_scalar", r, src_ap, invs_ap, 127.49,
                          op0=AOP.mult, op1=AOP.min, waits=list(waits))
                m2 = P.op("ve", "tensor_scalar", r, r, -128.49, 129.5,
                          op0=AOP.max, op1=AOP.add, waits=[m1])
                m3 = P.op("ve", "memset", acc, -129.0, waits=list(acc_waits))
                prev = m2
                for k in range(8, -1, -1):
                    th = float(2 ** k)
                    mb = P.op("ve", "tensor_scalar", bt, r, th, -th,
                              op0=AOP.is_ge, op1=AOP.mult, waits=[prev])
                    P.op("ve", "tensor_tensor", r, r, bt, op=AOP.add, waits=[mb])
                    prev = P.op("ve", "tensor_tensor", acc, acc, bt, op=AOP.subtract)
                return prev, P_(6144, n)

            def transpose_out(ib, ncc, dst_fn, waits):
                last_ev = P.mark("ve")
                last_dma = P.mark("c_gst")
                last_t = None
                for cc4 in range(0, ncc, 4):
                    nblk = min(4, ncc - cc4)
                    mts = None
                    for jj in range(nblk):
                        cc = cc4 + jj
                        mts = P.op("te", "transpose",
                                   psb[4][:, jj * 128:(jj + 1) * 128],
                                   ib[:, cc * 128:(cc + 1) * 128], ident[:],
                                   waits=list(waits) + [last_ev])
                    stage = s_wstage
                    mev = P.op("ve", "tensor_copy", stage[:, 0:nblk * 128],
                               psb[4][:, 0:nblk * 128],
                               waits=[mts, last_dma])
                    last_ev = mev
                    last_t = mts
                    for jj in range(nblk):
                        last_dma = P.op("gp", "dma_start", sem="c_gst", amt=16,
                                        out=dst_fn(cc4 + jj),
                                        in_=stage[:, jj * 128:(jj + 1) * 128],
                                        waits=[mev])
                return last_t

            def quant_gateup(wmat, scale_tile, outT):
                nonlocal acc_free
                buf_free = [P.mark("ve"), P.mark("ve")]
                for rc in range(NIC):
                    par = rc % 2
                    wbuf = P_(par * 2048, 1024)
                    mld = P.op("sy", "dma_start", sem="c_w%d" % par, amt=16,
                               out=wbuf, in_=wmat[rc * 128:(rc + 1) * 128, :],
                               waits=[buf_free[par]])
                    mrm = P.op("ve", "tensor_reduce", smal[:, 8:9], wbuf,
                               axis=AX.X, op=AOP.max, apply_absolute_value=True,
                               waits=[mld])
                    msc = P.op("ve", "tensor_scalar", scale_tile[:, rc:rc + 1],
                               smal[:, 8:9], 1.0 / QMAX, None, op0=AOP.mult, waits=[mrm])
                    miv = P.op("ve", "reciprocal", smal[:, 9:10],
                               scale_tile[:, rc:rc + 1], waits=[msc])
                    mq, accv = ladder_w(wbuf, 1024, smal[:, 9:10], [acc_free], [miv])
                    buf_free[par] = mq
                    acc_free = transpose_out(accv, 8,
                                             lambda cc, rc=rc: outT[cc * 128:(cc + 1) * 128,
                                                                    rc * 128:(rc + 1) * 128],
                                             [mq])

            acc_free = P.mark("te")
            quant_gateup(wg, wsg_t, wgq_s)
            quant_gateup(wu, wsu_t, wuq_s)

            for rc in range(8):
                wbuf0 = P_(0, 1408)
                wbuf1 = P_(2048, 1408)
                mld0 = P.op("sy", "dma_start", sem="c_w0", amt=16, out=wbuf0,
                            in_=wd[rc * 128:(rc + 1) * 128, 0:1408],
                            waits=[P.mark("ve")])
                mld1 = P.op("sy", "dma_start", sem="c_w1", amt=16, out=wbuf1,
                            in_=wd[rc * 128:(rc + 1) * 128, 1408:2816])
                mrm0 = P.op("ve", "tensor_reduce", smal[:, 8:9], wbuf0,
                            axis=AX.X, op=AOP.max, apply_absolute_value=True, waits=[mld0])
                mrm1 = P.op("ve", "tensor_reduce", smal[:, 10:11], wbuf1,
                            axis=AX.X, op=AOP.max, apply_absolute_value=True, waits=[mld1])
                mrm = P.op("ve", "tensor_tensor", smal[:, 8:9], smal[:, 8:9],
                           smal[:, 10:11], op=AOP.max, waits=[mrm0, mrm1])
                msc = P.op("ve", "tensor_scalar", wsd_t[:, rc:rc + 1], smal[:, 8:9],
                           1.0 / QMAX, None, op0=AOP.mult, waits=[mrm])
                miv = P.op("ve", "reciprocal", smal[:, 9:10], wsd_t[:, rc:rc + 1],
                           waits=[msc])
                for half in range(2):
                    c0 = half * 1408
                    wbuf = wbuf0 if half == 0 else wbuf1
                    mq, accv = ladder_w(wbuf, 1408, smal[:, 9:10], [acc_free], [miv])
                    acc_free = transpose_out(accv, 11,
                                             lambda cc, c0=c0, rc=rc: wdq_s[
                                                 c0 + cc * 128:c0 + (cc + 1) * 128,
                                                 rc * 128:(rc + 1) * 128], [mq])

            m_wsd = P.mark("ve")
            mts = None
            for j in range(8):
                bank = psb[4] if j < 4 else psb[5]
                jj = j % 4
                mts = P.op("te", "transpose", bank[:1, jj * 128:(jj + 1) * 128],
                           wsd_t[:, j:j + 1], ident[:], waits=[m_wsd],
                           nopc=TR_NOP)
            mrow0 = P.op("ve", "tensor_copy", P_(11776, 1024)[:1, 0:512],
                         psb[4][:1, 0:512], waits=[mts])
            mrow = P.op("ve", "tensor_copy", P_(11776, 1024)[:1, 512:1024],
                        psb[5][:1, 0:512], waits=[mts])
            for hh in range(2):
                mb = P.op("te", "matmul", psb[6][:, :], ones[:1, 0:128],
                          P_(11776, 1024)[:1, hh * 512:(hh + 1) * 512],
                          waits=[mrow, P.mark("ve")], start=True, stop=True)
                P.op("ve", "tensor_copy", wdsb[:, hh * 512:(hh + 1) * 512], psb[6][:, :],
                     waits=[mb])
            P.barrier()
            if debug:
                for r in range(NIC):
                    mldw = P.op("sy", "dma_start", sem="c_dbg", amt=16,
                                out=P_(0, 512).bitcast(BF16)[:, 0:1024],
                                in_=wdq_s[r * 128:(r + 1) * 128, :],
                                waits=[P.mark("c_gdbg"), P.mark("ve")])
                    mcvw = P.op("ve", "tensor_copy", P_(1024, 1024),
                                P_(0, 512).bitcast(BF16)[:, 0:1024], waits=[mldw])
                    P.op("gp", "dma_start", sem="c_gdbg", amt=16,
                         out=dbg["d_wdq"][r * 128:(r + 1) * 128, :], in_=P_(1024, 1024),
                         waits=[mcvw])
                P.barrier()

            bvb = P_(20480, 1024)
            mbv = P.op("sy", "dma_start", sem="c_ini", amt=16, out=bvb[:1, :],
                       in_=ba[2 * C:3 * C].unsqueeze(0))
            for hh in range(2):
                mb = P.op("te", "matmul", psb[6][:, :], ones[:1, 0:128],
                          bvb[:1, hh * 512:(hh + 1) * 512],
                          waits=[mbv, P.mark("ve")], start=True, stop=True)
                P.op("ve", "tensor_copy", bvb[:, hh * 512:(hh + 1) * 512], psb[6][:, :],
                     waits=[mb])
            m_bvb = P.mark("ve")

            def ln_tile(xt_ap, waits):
                sq = P_(4096, 1024)
                mS = P.op("ve", "tensor_reduce", smal[:, 12:13], xt_ap, axis=AX.X,
                          op=AOP.add, waits=list(waits))
                mM = P.op("ve", "tensor_scalar", smal[:, 12:13], smal[:, 12:13],
                          1.0 / C, None, op0=AOP.mult, waits=[mS])
                mQ = P.op("ve", "tensor_tensor", sq, xt_ap, xt_ap, op=AOP.mult,
                          waits=list(waits))
                mS2 = P.op("ve", "tensor_reduce", smal[:, 13:14], sq, axis=AX.X,
                           op=AOP.add, waits=[mQ])
                mE2 = P.op("ve", "tensor_scalar", smal[:, 13:14], smal[:, 13:14],
                           1.0 / C, None, op0=AOP.mult, waits=[mS2])
                mMM = P.op("ve", "tensor_tensor", smal[:, 14:15], smal[:, 12:13],
                           smal[:, 12:13], op=AOP.mult, waits=[mM])
                mV = P.op("ve", "tensor_tensor", smal[:, 13:14], smal[:, 13:14],
                          smal[:, 14:15], op=AOP.subtract, waits=[mE2, mMM])
                mVe = P.op("ve", "tensor_scalar", smal[:, 13:14], smal[:, 13:14],
                           1e-5, None, op0=AOP.add, waits=[mV])
                mR = P.op("ve", "reciprocal", smal[:, 13:14], smal[:, 13:14], waits=[mVe])
                mRs = P.op("sc", "activation", out=smal[:, 15:16], in_=smal[:, 13:14],
                           func=AF.Sqrt, waits=[mR])
                return P.op("ve", "tensor_scalar", xt_ap, xt_ap, smal[:, 12:13],
                            smal[:, 15:16], op0=AOP.subtract, op1=AOP.mult,
                            waits=[mRs])

            def transpose_xh(xt_tiles_marks):
                marks = []
                last_ev = P.mark("ve")
                for fc in range(8):
                    mts = None
                    for tt in range(4):
                        mts = P.op("te", "transpose", psb[4][:, tt * 128:(tt + 1) * 128],
                                   P_(tt * 1024, 1024)[:, fc * 128:(fc + 1) * 128],
                                   ident[:],
                                   waits=[xt_tiles_marks[tt], last_ev])
                    mev = P.op("ve", "tensor_copy", P_(5120 + fc * 512, 512),
                               psb[4][:, :], waits=[mts])
                    last_ev = mev
                    marks.append(mev)
                return marks

            wa_free = [P.mark("te"), P.mark("te")]
            kq_stage_free = [P.mark("c_gkq0"), P.mark("c_gkq1")]
            v_stage_free = [P.mark("c_gv0"), P.mark("c_gv1")]
            vw_free = [P.mark("te"), P.mark("te")]
            v_ps_free = [P.mark("ve"), P.mark("ve")]
            m_abs_psA = [P.mark("sc"), P.mark("sc")]

            def qkv_pass(src_ap, tb, do_q, do_kv, tb_cols):
                mx = []
                for tt in range(4):
                    mx.append(P.op("sy", "dma_start", sem="c_x", amt=16,
                                   out=P_(tt * 1024, 1024),
                                   in_=src_ap[tb * 512 + tt * 128:tb * 512 + (tt + 1) * 128, :],
                                   waits=[P.mark("te"), P.mark("ve")]))
                mln = []
                prev = P.mark("ve")
                for tt in range(4):
                    m = ln_tile(P_(tt * 1024, 1024), [mx[3], prev])
                    mln.append(m)
                    prev = m
                mT = transpose_xh(mln)
                chunks = (list(range(8)) if do_q else []) + (list(range(8, 16)) if do_kv else [])
                for cchunk in chunks:
                    pv = cchunk % 2
                    pbk = psb[0 + pv]
                    mmm = None
                    for fc in range(8):
                        mmm = P.op("te", "matmul", pbk[:, :],
                                   P_(9216 + fc * 1024, 1024)[:, (cchunk % 8) * 128:
                                                              (cchunk % 8) * 128 + 128],
                                   P_(5120 + fc * 512, 512),
                                   waits=[m_wk, mT[fc], m_abs_psA[pv]],
                                   start=(fc == 0), stop=(fc == 7))
                    stg = P_(18432 + pv * 512, 512)
                    if cchunk >= 8:
                        kc_i = cchunk - 8
                        mev = P.op("sc", "activation", out=stg, in_=pbk[:, :],
                                   func=AF.Identity, bias=ba_t[:, cchunk:cchunk + 1],
                                   accum_out=ksump[:, kc_i * 8 + tb:kc_i * 8 + tb + 1],
                                   waits=[mmm, kq_stage_free[pv]])
                        dst = kT_s[kc_i * 128:(kc_i + 1) * 128, tb_cols]
                    else:
                        mev = P.op("sc", "activation", out=stg, in_=pbk[:, :],
                                   func=AF.Identity, bias=ba_t[:, cchunk:cchunk + 1],
                                   waits=[mmm, kq_stage_free[pv]])
                        dst = qT_s[cchunk * 128:(cchunk + 1) * 128, tb_cols]
                    m_abs_psA[pv] = mev
                    kq_stage_free[pv] = P.op("gp", "dma_start", sem="c_gkq%d" % pv,
                                             amt=16, out=dst, in_=stg, waits=[mev])
                if do_kv:
                    for tt in range(4):
                        for vh in range(2):
                            pv = (tt * 2 + vh) % 2
                            pbv = psb[2 + pv]
                            wslot = P_(17408 + pv * 512, 512)
                            mmm = None
                            for fc in range(8):
                                mws = P.op("sy", "dma_start", sem="c_vw%d" % pv, amt=16,
                                           out=wslot,
                                           in_=Wa[fc * 128:(fc + 1) * 128,
                                                  2 * C + vh * 512:2 * C + (vh + 1) * 512],
                                           waits=[vw_free[pv], P.mark("te")])
                                mmm = P.op("te", "matmul", pbv[:, :],
                                           P_(5120 + fc * 512, 512)[:, tt * 128:(tt + 1) * 128],
                                           wslot,
                                           waits=[mws, mT[fc], v_ps_free[pv]],
                                           start=(fc == 0), stop=(fc == 7))
                            vw_free[pv] = mmm
                            stg = P_(19456 + pv * 512, 512)
                            mev = P.op("ve", "tensor_tensor", stg, pbv[:, :],
                                       bvb[:, vh * 512:(vh + 1) * 512], op=AOP.add,
                                       waits=[mmm, m_bvb, v_stage_free[pv]])
                            v_ps_free[pv] = mev
                            v_stage_free[pv] = P.op(
                                "gp", "dma_start", sem="c_gv%d" % pv, amt=16,
                                out=vT_s[tb * 512 + tt * 128:tb * 512 + (tt + 1) * 128,
                                         vh * 512:(vh + 1) * 512],
                                in_=stg, waits=[mev])

            m_wk = None
            for fc in range(8):
                m_wk = P.op("sy", "dma_start", sem="c_wa0", amt=16,
                            out=P_(9216 + fc * 1024, 1024),
                            in_=Wa[fc * 128:(fc + 1) * 128, C:2 * C],
                            waits=[P.mark("ve"), P.mark("te")])
            for tb in range(NTB):
                qkv_pass(x_in, tb, False, True, slice(tb * 512, (tb + 1) * 512))
            for fc in range(8):
                m_wk = P.op("sy", "dma_start", sem="c_wa0", amt=16,
                            out=P_(9216 + fc * 1024, 1024),
                            in_=Wa[fc * 128:(fc + 1) * 128, 0:C],
                            waits=[P.mark("te")])
            for tb in range(4):
                qkv_pass(xq_in, tb, True, False, slice(tb * 512, (tb + 1) * 512))

            mkm = []
            for kc_i in range(8):
                mk1 = P.op("ve", "tensor_reduce", kmean[:, kc_i:kc_i + 1],
                           ksump[:, kc_i * 8:(kc_i + 1) * 8], axis=AX.X, op=AOP.add,
                           waits=[P.mark("sc")])
                mkm.append(P.op("ve", "tensor_scalar", kmean[:, kc_i:kc_i + 1],
                                kmean[:, kc_i:kc_i + 1], 1.0 / T, None,
                                op0=AOP.mult, waits=[mk1]))
            m_km = P.op("gp", "dma_start", sem="c_gkm", amt=16, out=km_s[:, :],
                        in_=kmean[:, 0:8], waits=mkm)

            if debug:
                P.barrier()
                mkd = P.op("ve", "tensor_copy", P_(8192, 8), kmean[:, 0:8],
                           waits=[m_km])
                P.op("gp", "dma_start", sem="c_gdbg", amt=16, out=dbg["d_scal"][:, 8:16],
                     in_=P_(8192, 8), waits=[mkd])
                for r in range(8):
                    mld = P.op("sy", "dma_start", sem="c_dbg", amt=16, out=P_(0, 2048),
                               in_=qT_s[r * 128:(r + 1) * 128, :],
                               waits=[P.mark("c_gdbg"), P.mark("ve")])
                    P.op("gp", "dma_start", sem="c_gdbg", amt=16,
                         out=dbg["d_qT"][r * 128:(r + 1) * 128, :], in_=P_(0, 2048),
                         waits=[mld])
                for r in range(8):
                    mld = P.op("sy", "dma_start", sem="c_dbg", amt=16, out=P_(0, 4096),
                               in_=kT_s[r * 128:(r + 1) * 128, :], waits=[P.mark("c_gdbg")])
                    P.op("gp", "dma_start", sem="c_gdbg", amt=16,
                         out=dbg["d_kT"][r * 128:(r + 1) * 128, :], in_=P_(0, 4096),
                         waits=[mld])
                for r in range(32):
                    mld = P.op("sy", "dma_start", sem="c_dbg", amt=16, out=P_(0, 1024),
                               in_=vT_s[r * 128:(r + 1) * 128, :], waits=[P.mark("c_gdbg")])
                    P.op("gp", "dma_start", sem="c_gdbg", amt=16,
                         out=dbg["d_vT"][r * 128:(r + 1) * 128, :], in_=P_(0, 1024),
                         waits=[mld])


            P.barrier()

            kbuf = P_(0, 4096)
            ascr = P_(17920, 512)
            madp = P_(18432, 32)
            mad4 = P_(18464, 4)
            wch1 = P_(18468, 4)
            wch2 = P_(18472, 4)
            s4p = P_(18476, 4)
            wden = P_(18480, 4)
            w4 = P_(18484, 4)
            s4row = P_(18496, 512)
            wrow = P_(19008, 512)
            yt = P_(19520, 512)
            kmcur = P_(20032, 1)
            wbt = P_(20096, 512)

            mvo = []
            for hp in range(2):
                vb = P_(5120 + hp * 4096, 4096).rearrange("p (k c) -> p k c", c=128)
                m1_ = P.op("ve", "memset", vb[:, :, 0:1], 1.0)
                mvo.append(P.op("ve", "memset", vb[:, :, 1:64], 0.0, waits=[m1_]))

            km_free = P.mark("ve")
            bband_free = [P.mark("ve"), P.mark("ve"), P.mark("ve")]
            kbuf_free = [P.mark("te"), P.mark("ve")]
            q_free = [P.mark("te"), P.mark("te")]
            v_free = [mvo[0], mvo[1]]
            u4_free = [P.mark("te"), P.mark("te"), P.mark("te")]
            mask_free = P.mark("ve")
            yt_free = P.mark("c_gyt")
            s4_free = P.mark("te")
            s5_free = P.mark("ve")
            wbt_free = P.mark("ve")

            def emit_vload(h_):
                hp_ = h_ % 2
                vslot_ = P_(5120 + hp_ * 4096, 4096)
                return P.op("gp", "dma_start", sem="c_v%d" % hp_, amt=16,
                            out=vslot_.rearrange("p (k c) -> p k c", c=128)[:, :, 64:128],
                            in_=vT_s.rearrange("(k p) c -> p k c",
                                               p=128)[:, :, h_ * 64:(h_ + 1) * 64],
                            waits=[v_free[hp_]])

            def emit_mload(j_):
                return P.op("gp", "dma_start", sem="c_msk", amt=16,
                            out=mtiles[:, j_ * 4096:(j_ + 1) * 4096].rearrange(
                                "p (a c) -> p a c", c=512),
                            in_=m01_in[j_].transpose([1, 0, 2]))

            mvl_marks = [None] * NH
            mmask4 = [None] * 4
            mmask4[0] = emit_mload(0)
            mvl_marks[0] = emit_vload(0)
            mmask4[1] = emit_mload(1)
            mvl_marks[1] = emit_vload(1)
            mmask4[2] = emit_mload(2)
            mmask4[3] = emit_mload(3)

            for h in range(NH):
                hp = h % 2
                mkm_l = P.op("sy", "dma_start", sem="c_km", amt=16,
                             out=kmcur[:64, :],
                             in_=km_s[(h % 2) * 64:(h % 2) * 64 + 64,
                                      h // 2:h // 2 + 1],
                             waits=[km_free, m_km],
                             allow_slow_non_contiguous=True)
                mkl = P.op("sy", "dma_start", sem="c_k", amt=16,
                           out=kbuf[:64, :],
                           in_=kT_s[h * 64:h * 64 + 64, :],
                           waits=kbuf_free)
                mkc = P.op("ve", "tensor_scalar", kbuf[:64, :], kbuf[:64, :],
                           kmcur[:64, :], None, op0=AOP.subtract,
                           waits=[mkl, mkm_l])
                km_free = mkc
                vslot = P_(5120 + hp * 4096, 4096)
                mvl = mvl_marks[h]
                for j in range(4):
                    kcd = KCDM[j]
                    mmask = mmask4[j]
                    qsl = P_(4096 + (j % 2) * 512, 512)
                    mql = P.op("sy", "dma_start", sem="c_q%d" % (j % 2), amt=16,
                               out=qsl[:64, :],
                               in_=qT_s[h * 64:h * 64 + 64, j * 512:(j + 1) * 512],
                               waits=[q_free[j % 2]])

                    mabs = [None, None]
                    last_abs = P.mark("sc")
                    a_steps = [(qt, kb) for qt in range(4) for kb in range(8)]
                    a_pos = 0

                    def emit_a(nsteps):
                        nonlocal a_pos, last_abs
                        for _ in range(nsteps):
                            if a_pos >= len(a_steps):
                                return
                            qt, kb = a_steps[a_pos]
                            pa = a_pos % 2
                            a_pos += 1
                            pba = psb[0 + pa]
                            w = [mql, mkc]
                            if mabs[pa] is not None:
                                w.append(mabs[pa])
                            mA = P.op("te", "matmul", pba[:, :],
                                      qsl[:64, qt * 128:(qt + 1) * 128],
                                      kbuf[:64, kb * 512:(kb + 1) * 512],
                                      waits=w, start=True, stop=True,
                                      nopc=K64_NOP)
                            mabs[pa] = P.op("sc", "activation", out=ascr, in_=pba[:, :],
                                            func=AF.Abs,
                                            accum_out=madp[:, qt * 8 + kb:qt * 8 + kb + 1],
                                            waits=[mA, last_abs])
                            last_abs = mabs[pa]

                    mq2 = [None, None, None]
                    mu4 = [None, None, None]
                    mav = None
                    bbanks = [2, 3, 7]
                    for kc in range(kcd):
                        pb_ = kc % 3
                        pbs = psb[bbanks[pb_]]
                        etile = P_(13312 + pb_ * 512, 512)
                        q2t = P_(14848 + pb_ * 512, 512)
                        u4t = P_(16384 + pb_ * 512, 512)
                        w = [mkc, mql]
                        if mq2[pb_] is not None:
                            w.append(mq2[pb_])
                        else:
                            w.append(bband_free[pb_])
                        mB = P.op("te", "matmul", pbs[:, :],
                                  kbuf[:64, kc * 128:(kc + 1) * 128],
                                  qsl[:64, :],
                                  waits=w, start=True, stop=True, nopc=K64_NOP)
                        emit_a(2 if kcd <= 16 else 1)
                        woff = kc - (kcd - 8)
                        we = [mB]
                        if mq2[pb_] is not None:
                            we.append(mq2[pb_])
                        if woff >= 0:
                            me = P.op("ve", "tensor_tensor", etile, pbs[:, :],
                                      mtiles.rearrange("p (a c) -> p a c", c=512)[:, j * 8 + woff, :],
                                      op=AOP.mult, waits=[mmask] + we)
                        else:
                            me = P.op("ve", "tensor_copy", etile, pbs[:, :], waits=we)
                        mq2[pb_] = P.op("ve", "scalar_tensor_tensor", q2t, pbs[:, :],
                                        0.0, etile, op0=AOP.max, op1=AOP.mult,
                                        waits=[me])
                        if debug and j == 0 and h == 0 and kc == 0:
                            P.op("gp", "dma_start", sem="c_gdbg", amt=16,
                                 out=dbg["d_e"], in_=etile, waits=[mq2[pb_]])
                            P.op("gp", "dma_start", sem="c_gdbg", amt=16,
                                 out=dbg["d_q2"], in_=q2t, waits=[mq2[pb_]])
                        mu4[pb_] = P.op("sc", "activation", out=u4t, in_=q2t,
                                        func=AF.Square,
                                        waits=[mq2[pb_], u4_free[pb_]])
                        if debug and j == 0 and h == 0 and kc == 0:
                            P.op("gp", "dma_start", sem="c_gdbg", amt=16,
                                 out=dbg["d_u4"], in_=u4t, waits=[mu4[pb_]])
                        mav = P.op("te", "matmul", psb[4][:, :],
                                   vslot.rearrange("p (k c) -> p k c", c=128)[:, kc, :],
                                   u4t,
                                   waits=[mu4[pb_], mvl, s4_free,
                                          P.mark("c_gdbg") if debug else ("te", 0)],
                                   start=(kc == 0), stop=(kc == kcd - 1))
                        u4_free[pb_] = mav
                    emit_a(len(a_steps))
                    kbuf_free = [mabs[0], mabs[1], mav, mkc]
                    q_free[j % 2] = mav
                    mmadr = []
                    for qt in range(4):
                        mmadr.append(P.op("ve", "tensor_reduce", mad4[:, qt:qt + 1],
                                          madp[:, qt * 8:(qt + 1) * 8], axis=AX.X,
                                          op=AOP.add, waits=[mabs[0], mabs[1]]))
                    mmad = P.op("ve", "tensor_scalar", mad4, mad4, 1.0 / T, 1e-6,
                                op0=AOP.mult, op1=AOP.add, waits=mmadr)
                    mw1 = P.op("ve", "tensor_scalar", wch1, mad4, invg[:, 0:1], None,
                               op0=AOP.mult, waits=[mmad, m_invg])
                    mw2 = P.op("ve", "tensor_tensor", wch2, wch1, wch1, op=AOP.mult,
                               waits=[mw1])
                    mw3 = P.op("ve", "tensor_tensor", wch1, wch2, wch2, op=AOP.mult,
                               waits=[mw2])
                    mw4 = P.op("ve", "tensor_scalar", wch1, wch1, 1e-6, None,
                               op0=AOP.mult, waits=[mw3])

                    ms4 = P.op("ve", "tensor_copy", s4row[:1, :], psb[4][0:1, :],
                               waits=[mav])
                    mt4 = None
                    for qt in range(4):
                        mt4 = P.op("te", "transpose", psb[5][:, qt:qt + 1],
                                   s4row[:1, qt * 128:(qt + 1) * 128], ident[:1, :1],
                                   waits=[ms4, s5_free])
                    ms4p = P.op("ve", "tensor_copy", s4p, psb[5][:, 0:4], waits=[mt4])
                    mden = P.op("ve", "tensor_tensor", wden, s4p, wch1, op=AOP.add,
                                waits=[ms4p, mw4])
                    mw4v = P.op("ve", "reciprocal", w4, wden, waits=[mden])
                    mtw = None
                    for qt in range(4):
                        mtw = P.op("te", "transpose", psb[5][:1, qt * 128:(qt + 1) * 128],
                                   w4[:, qt:qt + 1], ident[:],
                                   waits=[mw4v, ms4p])
                    mwr = P.op("ve", "tensor_copy", wrow[:1, :], psb[5][:1, 0:512],
                               waits=[mtw])
                    mwb = P.op("te", "matmul", psb[6][:, :], ones[:1, 0:128],
                               wrow[:1, :], waits=[mwr, wbt_free], start=True,
                               stop=True)
                    mwbe = P.op("ve", "tensor_copy", wbt, psb[6][:, :], waits=[mwb])
                    wbt_free = mwbe
                    s5_free = mwr
                    for pp in range(3):
                        if mq2[pp] is not None:
                            bband_free[pp] = mq2[pp]
                    if debug and j == 0 and h == 0:
                        mdn1 = P.op("ve", "tensor_copy", P_(19200, 4), s4p, waits=[ms4p])
                        mdn2 = P.op("ve", "tensor_copy", P_(19204, 4), mad4, waits=[mmad, mdn1])
                        mdn3 = P.op("ve", "tensor_copy", P_(19208, 4), wch1, waits=[mw4, mdn2])
                        mdn4 = P.op("ve", "tensor_copy", P_(19212, 4), w4, waits=[mw4v, mdn3])
                        P.op("gp", "dma_start", sem="c_gdbg", amt=16,
                             out=dbg["d_nrm"][:, 0:8],
                             in_=P_(19200, 8), waits=[mdn1, mdn2, mdn3, mdn4])
                    myt = P.op("ve", "tensor_tensor", yt[64:128, :], psb[4][64:128, :],
                               wbt[64:128, :], op=AOP.mult,
                               waits=[mwbe, yt_free] + ([P.mark("c_gdbg")] if debug else []))
                    s4_free = myt
                    yt_free = P.op("gp", "dma_start", sem="c_gyt", amt=16,
                                   out=yT_s[h * 64:(h + 1) * 64, j * 512:(j + 1) * 512],
                                   in_=yt[64:128, :], waits=[myt])
                v_free[hp] = mav
                if h + 2 < NH:
                    mvl_marks[h + 2] = emit_vload(h + 2)

            if debug:
                P.barrier()
                for r in range(8):
                    mld = P.op("sy", "dma_start", sem="c_dbg", amt=16, out=P_(0, 2048),
                               in_=yT_s[r * 128:(r + 1) * 128, :], waits=[P.mark("c_gdbg")])
                    P.op("gp", "dma_start", sem="c_gdbg", amt=16,
                         out=dbg["d_yT"][r * 128:(r + 1) * 128, :], in_=P_(0, 2048),
                         waits=[mld])

            P.barrier()

            bpb = P_(14336, 1024)
            mbp = P.op("sy", "dma_start", sem="c_ini", amt=16, out=bpb[:1, :],
                       in_=bp.unsqueeze(0))
            for hh in range(2):
                mb = P.op("te", "matmul", psb[6][:, :], ones[:1, 0:128],
                          bpb[:1, hh * 512:(hh + 1) * 512],
                          waits=[mbp, P.mark("ve")], start=True, stop=True)
                P.op("ve", "tensor_copy", bpb[:, hh * 512:(hh + 1) * 512], psb[6][:, :],
                     waits=[mb])
            m_bpb = P.mark("ve")
            mwp = []
            for cc in range(8):
                mwp.append(P.op("sy", "dma_start", sem="c_wp", amt=16,
                                out=P_(cc * 1024, 1024),
                                in_=Wp[cc * 128:(cc + 1) * 128, :],
                                waits=[P.mark("te"), P.mark("ve")]))
            yst_free = [P.mark("te"), P.mark("te")]
            stage_free = [P.mark("c_gxp0"), P.mark("c_gxp1")]
            prev_evac = P.mark("ve")
            for pass4 in range(4):
                mxr = []
                for tt in range(4):
                    mxr.append(P.op("sy", "dma_start", sem="c_xr", amt=16,
                                    out=P_(9216 + tt * 1024, 1024),
                                    in_=xq_in[pass4 * 512 + tt * 128:
                                              pass4 * 512 + (tt + 1) * 128, :],
                                    waits=[prev_evac]))
                mmm = [None] * 8
                for cc in range(8):
                    pv = cc % 2
                    yslot = P_(8192 + pv * 512, 512)
                    myl = P.op("sy", "dma_start", sem="c_y%d" % pv, amt=16,
                               out=yslot,
                               in_=yT_s[cc * 128:(cc + 1) * 128,
                                        pass4 * 512:(pass4 + 1) * 512],
                               waits=[yst_free[pv]])
                    mlast = None
                    for tt in range(4):
                        for oh in range(2):
                            pbi = tt * 2 + oh
                            w = [myl, mwp[7]]
                            if mmm[pbi] is not None:
                                w.append(mmm[pbi])
                            elif pass4 > 0:
                                w.append(prev_evac)
                            mlast = P.op("te", "matmul", psb[pbi][:, :],
                                         yslot[:, tt * 128:(tt + 1) * 128],
                                         P_(cc * 1024, 1024)[:, oh * 512:(oh + 1) * 512],
                                         waits=w, start=(cc == 0), stop=(cc == 7))
                            mmm[pbi] = mlast
                    yst_free[pv] = mlast
                for tt in range(4):
                    for oh in range(2):
                        pbi = tt * 2 + oh
                        pv = pbi % 2
                        stg = P_(13312 + pv * 512, 512)
                        m1 = P.op("ve", "tensor_tensor", stg, psb[pbi][:, :],
                                  bpb[:, oh * 512:(oh + 1) * 512], op=AOP.add,
                                  waits=[mmm[pbi], m_bpb, stage_free[pv]])
                        m2 = P.op("ve", "tensor_tensor", stg, stg,
                                  P_(9216 + tt * 1024, 1024)[:, oh * 512:(oh + 1) * 512],
                                  op=AOP.add, waits=[m1, mxr[3]])
                        stage_free[pv] = P.op("gp", "dma_start", sem="c_gxp%d" % pv, amt=16,
                                              out=xp_s[pass4 * 512 + tt * 128:
                                                       pass4 * 512 + (tt + 1) * 128,
                                                       oh * 512:(oh + 1) * 512],
                                              in_=stg, waits=[m2])
                prev_evac = P.mark("ve")

            if debug:
                P.barrier()
                for r in range(16):
                    mld = P.op("sy", "dma_start", sem="c_dbg", amt=16, out=P_(0, 1024),
                               in_=xp_s[r * 128:(r + 1) * 128, :], waits=[P.mark("c_gdbg")])
                    P.op("gp", "dma_start", sem="c_gdbg", amt=16,
                         out=dbg["d_xp"][r * 128:(r + 1) * 128, :], in_=P_(0, 1024),
                         waits=[mld])

            P.barrier()

            m2c = P_(5120, 16)
            r2c = P_(5136, 16)
            rsc = P_(5152, 16)
            gacc = P_(5220, 1)
            mgacc = P.op("ve", "memset", gacc, 0.0)
            xt = P_(0, 1024)
            xt_free = P.mark("ve")
            for ti in range(16):
                mld = P.op("sy", "dma_start", sem="c_x", amt=16, out=xt,
                           in_=xp_s[ti * 128:(ti + 1) * 128, :], waits=[xt_free])
                sq = P_(1024, 1024)
                mS = P.op("ve", "tensor_reduce", m2c[:, ti:ti + 1], xt, axis=AX.X,
                          op=AOP.add, waits=[mld])
                mM = P.op("ve", "tensor_scalar", m2c[:, ti:ti + 1], m2c[:, ti:ti + 1],
                          1.0 / C, None, op0=AOP.mult, waits=[mS])
                mQ = P.op("ve", "tensor_tensor", sq, xt, xt, op=AOP.mult, waits=[mld])
                mS2 = P.op("ve", "tensor_reduce", smal[:, 12:13], sq, axis=AX.X,
                           op=AOP.add, waits=[mQ])
                mE2 = P.op("ve", "tensor_scalar", smal[:, 12:13], smal[:, 12:13],
                           1.0 / C, None, op0=AOP.mult, waits=[mS2])
                mMM = P.op("ve", "tensor_tensor", smal[:, 13:14], m2c[:, ti:ti + 1],
                           m2c[:, ti:ti + 1], op=AOP.mult, waits=[mM])
                mV = P.op("ve", "tensor_tensor", smal[:, 12:13], smal[:, 12:13],
                          smal[:, 13:14], op=AOP.subtract, waits=[mE2, mMM])
                mVe = P.op("ve", "tensor_scalar", smal[:, 12:13], smal[:, 12:13],
                           1e-5, None, op0=AOP.add, waits=[mV])
                mR = P.op("ve", "reciprocal", smal[:, 12:13], smal[:, 12:13], waits=[mVe])
                mRs = P.op("sc", "activation", out=r2c[:, ti:ti + 1], in_=smal[:, 12:13],
                           func=AF.Sqrt, waits=[mR])
                mdx = P.op("ve", "tensor_scalar", sq, xt, m2c[:, ti:ti + 1], None,
                           op0=AOP.subtract, accum_out=smal[:, 14:15], op1=AOP.max,
                           waits=[mM])
                mdn = P.op("ve", "tensor_scalar", sq, xt, m2c[:, ti:ti + 1], None,
                           op0=AOP.subtract, accum_out=smal[:, 15:16], op1=AOP.min,
                           waits=[mdx])
                mneg = P.op("ve", "tensor_scalar", smal[:, 15:16], smal[:, 15:16],
                            -1.0, None, op0=AOP.mult, waits=[mdn])
                mmx = P.op("ve", "tensor_tensor", smal[:, 14:15], smal[:, 14:15],
                           smal[:, 15:16], op=AOP.max, waits=[mneg])
                mdev = P.op("ve", "tensor_tensor", smal[:, 14:15], smal[:, 14:15],
                            r2c[:, ti:ti + 1], op=AOP.mult, waits=[mmx, mRs])
                mgacc = P.op("ve", "tensor_tensor", gacc, gacc, smal[:, 14:15],
                             op=AOP.max, waits=[mdev, mgacc])
                xt_free = mdn
            mcc_in = P.op("gp", "dma_start", sem="c_gcc", amt=16, out=cc1_in[:],
                          in_=gacc.squeeze(1), waits=[mgacc])
            mcc = P.op("gp", "collective_compute", "AllReduce", AOP.max,
                       sem="cc", amt=1, replica_groups=[list(range(8))],
                       ins=[cc1_in[:]], outs=[cc1_out[:]], waits=[mcc_in])
            mccb = P.op("gp", "dma_start", sem="c_gcc", amt=16, out=P_(5221, 1),
                        in_=cc1_out[:].unsqueeze(1), waits=[mcc])
            mgr = P.op("gp", "tensor_reduce", P_(5222, 1)[:1, :], P_(5221, 1),
                       sem="gpc", axis=AX.C, op=AOP.max, waits=[mccb])
            msig = P.op("sc", "activation", out=smal[:1, 5:6], in_=smal[:1, 1:2],
                        func=AF.Sigmoid, waits=[m_ld0])
            msx = P.op("ve", "tensor_tensor", smal[:1, 5:6], smal[:1, 5:6],
                       P_(5222, 1)[:1, :], op=AOP.mult, waits=[msig, mgr])
            msx2 = P.op("ve", "tensor_scalar", smal[:1, 5:6], smal[:1, 5:6],
                        1.0 / QMAX, None, op0=AOP.mult, waits=[msx])
            mix = P.op("ve", "reciprocal", smal[:1, 6:7], smal[:1, 5:6], waits=[msx2])
            mbx = bcast_scalar(smal[:1, 5:6], sxv[:, 1:2], [msx2])
            mbix = bcast_scalar(smal[:1, 6:7], sxv[:, 0:1], [mix])
            mrs = P.op("ve", "tensor_scalar", rsc, r2c, sxv[:, 0:1], None,
                       op0=AOP.mult, waits=[mbix])

            xt_free = P.mark("ve")
            xq_ev_free = P.mark("ve")
            acc_tr_free = P.mark("te")
            for ti in range(16):
                mld = P.op("sy", "dma_start", sem="c_x", amt=16, out=xt,
                           in_=xp_s[ti * 128:(ti + 1) * 128, :], waits=[xt_free])
                r = P_(2048, 1024)
                acc = P_(3072, 1024)
                bt = P_(4096, 1024)
                m1 = P.op("ve", "tensor_scalar", r, xt, m2c[:, ti:ti + 1],
                          rsc[:, ti:ti + 1], op0=AOP.subtract, op1=AOP.mult,
                          waits=[mld, mrs])
                m2_ = P.op("ve", "tensor_scalar", r, r, 127.49, -128.49,
                           op0=AOP.min, op1=AOP.max, waits=[m1])
                m3 = P.op("ve", "tensor_scalar", r, r, 129.5, None,
                          op0=AOP.add, waits=[m2_])
                macc = P.op("ve", "memset", acc, -129.0, waits=[acc_tr_free])
                prev = m3
                for k in range(8, -1, -1):
                    th = float(2 ** k)
                    mb_ = P.op("ve", "tensor_scalar", bt, r, th, -th,
                               op0=AOP.is_ge, op1=AOP.mult, waits=[prev])
                    P.op("ve", "tensor_tensor", r, r, bt, op=AOP.add, waits=[mb_])
                    prev = P.op("ve", "tensor_tensor", acc, acc, bt, op=AOP.subtract)
                xt_free = m1
                last_ev = xq_ev_free
                for hc4 in range(0, 8, 4):
                    mts = None
                    for jj in range(4):
                        hc = hc4 + jj
                        mts = P.op("te", "transpose",
                                   psb[4][:, jj * 128:(jj + 1) * 128],
                                   acc[:, hc * 128:(hc + 1) * 128], ident[:],
                                   waits=[prev, macc, last_ev])
                    mcps = []
                    for jj in range(4):
                        hc = hc4 + jj
                        mcps.append(P.op("ve", "tensor_copy",
                                         xqT[:, hc * TQ + ti * 128:hc * TQ + (ti + 1) * 128],
                                         psb[4][:, jj * 128:(jj + 1) * 128],
                                         waits=[mts]))
                    last_ev = mcps[-1]
                    acc_tr_free = mts
                xq_ev_free = last_ev

            if debug:
                P.barrier()
                for hc in range(8):
                    mcv = P.op("ve", "tensor_copy", P_(6144, 2048),
                               xqT[:, hc * TQ:(hc + 1) * TQ], waits=[P.mark("c_gdbg")])
                    P.op("gp", "dma_start", sem="c_gdbg", amt=16,
                         out=dbg["d_xqi"][hc * 128:(hc + 1) * 128, :], in_=P_(6144, 2048),
                         waits=[mcv])
                mss = P.mark("ve")
                P.op("gp", "dma_start", sem="c_gdbg", amt=16, out=dbg["d_scal"][:, 0:2],
                     in_=sxv[:, 0:2], waits=[mss])

            P.barrier()

            msg = P.op("ve", "tensor_scalar", P_(4096, NIC), wsg_t[:, 0:NIC],
                       sxv[:, 1:2], None, op0=AOP.mult, waits=[mbx])
            msu = P.op("ve", "tensor_scalar", P_(4160, NIC), wsu_t[:, 0:NIC],
                       sxv[:, 1:2], None, op0=AOP.mult, waits=[msg])
            sgv_t = P_(4096, NIC)
            suv_t = P_(4160, NIC)
            mhab = P.op("ve", "memset", habs[:], 0.0)
            wq_free = [P.mark("te"), P.mark("te")]
            hp_free = [P.mark("c_ghp0"), P.mark("c_ghp1")]
            gs_free = P.mark("ve")
            for ic in range(NIC):
                pv = ic % 2
                wgsl = s_wgu[:, pv * 2048:pv * 2048 + 1024]
                wusl = s_wgu[:, pv * 2048 + 1024:pv * 2048 + 2048]
                mlg = P.op("sy", "dma_start", sem="c_w%d" % pv, amt=16,
                           out=wgsl.rearrange("p (a c) -> p a c", c=128),
                           in_=wgq_s[:, ic * 128:(ic + 1) * 128].rearrange(
                               "(a p) c -> p a c", p=128),
                           waits=[wq_free[pv]])
                mlu = P.op("sy", "dma_start", sem="c_w%d" % pv, amt=16,
                           out=wusl.rearrange("p (a c) -> p a c", c=128),
                           in_=wuq_s[:, ic * 128:(ic + 1) * 128].rearrange(
                               "(a p) c -> p a c", p=128))
                mlast = None
                for tb in range(4):
                    pbg = psb[0 + (tb % 2) * 2]
                    pbu = psb[1 + (tb % 2) * 2]
                    mg = None
                    mu = None
                    for fc in range(8):
                        wg_w = [mlu]
                        if fc == 0:
                            wg_w.append(gs_free if tb < 2 and ic == 0 else P.mark("sc"))
                        mg = P.op("te", "matmul", pbg[:, :],
                                  wgsl[:, fc * 128:(fc + 1) * 128],
                                  xqT[:, fc * TQ + tb * 512:fc * TQ + (tb + 1) * 512],
                                  waits=wg_w, start=(fc == 0), stop=(fc == 7),
                                  nopc=BF16_NOP)
                        mu = P.op("te", "matmul", pbu[:, :],
                                  wusl[:, fc * 128:(fc + 1) * 128],
                                  xqT[:, fc * TQ + tb * 512:fc * TQ + (tb + 1) * 512],
                                  waits=[mlu] + ([P.mark("ve")] if fc == 0 else []),
                                  start=(fc == 0), stop=(fc == 7), nopc=BF16_NOP)
                    gs = P_(2048, 512)
                    mgs = P.op("sc", "activation", out=gs, in_=pbg[:, :], func=AF.Silu,
                               scale=sgv_t[:, ic:ic + 1], waits=[mg, msg, gs_free])
                    hpt = P_(2560 + (tb % 2) * 512, 512)
                    mhp = P.op("ve", "scalar_tensor_tensor", hpt, pbu[:, :],
                               suv_t[:, ic:ic + 1], gs, op0=AOP.mult, op1=AOP.mult,
                               waits=[mu, mgs, msu, hp_free[tb % 2]])
                    gs_free = mhp
                    mlast = mhp
                    mrm = P.op("ve", "tensor_reduce", smal[:, 12:13], hpt, axis=AX.X,
                               op=AOP.max, apply_absolute_value=True, waits=[mhp])
                    mhab = P.op("ve", "tensor_tensor", habs[:], habs[:], smal[:, 12:13],
                                op=AOP.max, waits=[mrm, mhab])
                    hp_free[tb % 2] = P.op("gp", "dma_start", sem="c_ghp%d" % (tb % 2),
                                           amt=16,
                                           out=hp_s[ic * 128:(ic + 1) * 128,
                                                    tb * 512:(tb + 1) * 512],
                                           in_=hpt, waits=[mhp])
                wq_free[pv] = mlast

            if debug:
                P.barrier()
                for r in range(NIC):
                    mld = P.op("sy", "dma_start", sem="c_dbg", amt=16, out=P_(6144, 2048),
                               in_=hp_s[r * 128:(r + 1) * 128, :], waits=[P.mark("c_gdbg")])
                    P.op("gp", "dma_start", sem="c_gdbg", amt=16,
                         out=dbg["d_hp"][r * 128:(r + 1) * 128, :], in_=P_(6144, 2048),
                         waits=[mld])

            mcc_in2 = P.op("gp", "dma_start", sem="c_gcc", amt=16, out=cc2_in[:],
                           in_=habs[:].squeeze(1), waits=[mhab])
            mcc2 = P.op("gp", "collective_compute", "AllReduce", AOP.max,
                        sem="cc", amt=1, replica_groups=[list(range(8))],
                        ins=[cc2_in[:]], outs=[cc2_out[:]], waits=[mcc_in2])
            mccb2 = P.op("gp", "dma_start", sem="c_gcc", amt=16, out=P_(5221, 1),
                         in_=cc2_out[:].unsqueeze(1), waits=[mcc2])
            mgr2 = P.op("gp", "tensor_reduce", P_(5222, 1)[:1, :], P_(5221, 1),
                        sem="gpc", axis=AX.C, op=AOP.max, waits=[mccb2])
            msg2 = P.op("sc", "activation", out=smal[:1, 7:8], in_=smal[:1, 2:3],
                        func=AF.Sigmoid, waits=[m_ld0])
            msh = P.op("ve", "tensor_tensor", smal[:1, 7:8], smal[:1, 7:8],
                       P_(5222, 1)[:1, :], op=AOP.mult, waits=[msg2, mgr2])
            msh2 = P.op("ve", "tensor_scalar", smal[:1, 7:8], smal[:1, 7:8],
                        1.0 / QMAX, None, op0=AOP.mult, waits=[msh])
            mih = P.op("ve", "reciprocal", smal[:1, 8:9], smal[:1, 7:8], waits=[msh2])
            mbh = bcast_scalar(smal[:1, 7:8], sxv[:, 3:4], [msh2])
            mbih = bcast_scalar(smal[:1, 8:9], sxv[:, 2:3], [mih])
            mwf = P.op("ve", "tensor_scalar", wdsb[:, :], wdsb[:, :], sxv[:, 3:4],
                       None, op0=AOP.mult, waits=[mbh])

            P.barrier()

            hq_free = [P.mark("te"), P.mark("te")]
            wd_free = [P.mark("te"), P.mark("te")]
            st_free = [P.mark("c_go0"), P.mark("c_go1")]
            lad_free = P.mark("ve")
            for tb in range(4):
                mdown = [None] * 8
                for ic in range(NIC):
                    pv = ic % 2
                    hsl = P_(pv * 512, 512)
                    mlh = P.op("sy", "dma_start", sem="c_h%d" % pv, amt=16, out=hsl,
                               in_=hp_s[ic * 128:(ic + 1) * 128, tb * 512:(tb + 1) * 512],
                               waits=[lad_free])
                    wdsl = s_wd2[:, pv * 1024:(pv + 1) * 1024]
                    mlw = P.op("sy", "dma_start", sem="c_wd%d" % pv, amt=16, out=wdsl,
                               in_=wdq_s[ic * 128:(ic + 1) * 128, :],
                               waits=[wd_free[pv]])
                    r = P_(1024, 512)
                    acc = P_(1536, 512)
                    bt = P_(2048, 512)
                    m1 = P.op("ve", "tensor_scalar", r, hsl, sxv[:, 2:3], 127.49,
                              op0=AOP.mult, op1=AOP.min, waits=[mlh, mbih])
                    m2_ = P.op("ve", "tensor_scalar", r, r, -128.49, 129.5,
                               op0=AOP.max, op1=AOP.add, waits=[m1])
                    macc = P.op("ve", "memset", acc, -129.0)
                    prev = m2_
                    for k in range(8, -1, -1):
                        th = float(2 ** k)
                        mb_ = P.op("ve", "tensor_scalar", bt, r, th, -th,
                                   op0=AOP.is_ge, op1=AOP.mult, waits=[prev])
                        P.op("ve", "tensor_tensor", r, r, bt, op=AOP.add, waits=[mb_])
                        prev = P.op("ve", "tensor_tensor", acc, acc, bt, op=AOP.subtract)
                    lad_free = m1
                    hqsl = s_hq[:, pv * 512:(pv + 1) * 512]
                    mhq = P.op("ve", "tensor_copy", hqsl, acc,
                               waits=[prev, macc, hq_free[pv]])
                    if debug and tb == 0 and ic == 0:
                        mcvh = P.op("ve", "tensor_copy", P_(8192, 512), hqsl,
                                    waits=[mhq])
                        P.op("gp", "dma_start", sem="c_gdbg", amt=16,
                             out=dbg["d_hq"], in_=P_(8192, 512), waits=[mcvh])
                        mcvs = P.op("ve", "tensor_copy", P_(8704, 4), sxv[:, 0:4],
                                    waits=[mcvh])
                        P.op("gp", "dma_start", sem="c_gdbg", amt=16,
                             out=dbg["d_scal"][:, 2:6], in_=P_(8704, 4), waits=[mcvs])
                    mlast = None
                    for tt in range(4):
                        for oh in range(2):
                            pbi = tt * 2 + oh
                            w = [mhq, mlw]
                            if mdown[pbi] is not None:
                                w.append(mdown[pbi])
                            elif tb > 0:
                                w.append(P.mark("ve"))
                            mlast = P.op("te", "matmul", psb[pbi][:, :],
                                         hqsl[:, tt * 128:(tt + 1) * 128],
                                         wdsl[:, oh * 512:(oh + 1) * 512],
                                         waits=w, start=(ic == 0), stop=(ic == NIC - 1),
                                         nopc=BF16_NOP)
                            mdown[pbi] = mlast
                    hq_free[pv] = mlast
                    wd_free[pv] = mlast
                mxr_free = P.mark("ve")
                for tt in range(4):
                    mxr = P.op("sy", "dma_start", sem="c_xr%s" % ("" if tt % 2 == 0 else "2"),
                               amt=16,
                               out=P_(4096 + (tt % 2) * 1024, 1024),
                               in_=xp_s[tb * 512 + tt * 128:tb * 512 + (tt + 1) * 128, :],
                               waits=[mxr_free if tt < 2 else P.mark("ve")])
                    for oh in range(2):
                        pbi = tt * 2 + oh
                        pv = pbi % 2
                        stg = P_(6144 + pv * 512, 512)
                        m1 = P.op("ve", "tensor_tensor", stg, psb[pbi][:, :],
                                  wdsb[:, oh * 512:(oh + 1) * 512], op=AOP.mult,
                                  waits=[mdown[pbi], mwf, st_free[pv]])
                        m2_ = P.op("ve", "tensor_tensor", stg, stg,
                                   P_(4096 + (tt % 2) * 1024, 1024)[:, oh * 512:(oh + 1) * 512],
                                   op=AOP.add, waits=[m1, mxr])
                        st_free[pv] = P.op("gp", "dma_start", sem="c_go%d" % pv, amt=16,
                                           out=out[tb * 512 + tt * 128:
                                                   tb * 512 + (tt + 1) * 128,
                                                   oh * 512:(oh + 1) * 512],
                                           in_=stg, waits=[m2_])

            P.wait("gp", [P.mark("c_go0"), P.mark("c_go1")])

        handles_names = ["te", "sc", "ve", "gp", "sy"]

        def run_engine(name, h):
            hh = dict.fromkeys(handles_names, None)
            hh[name] = h
            schedule(Prog(hh, sems))

        @block.sync
        def _(sync):
            run_engine("sy", sync)

        @block.tensor
        def _(tensor):
            run_engine("te", tensor)

        @block.scalar
        def _(scalar):
            run_engine("sc", scalar)

        @block.vector
        def _(vector):
            run_engine("ve", vector)

        @block.gpsimd
        def _(gpsimd):
            run_engine("gp", gpsimd)

    return nc, dbg


def make_masks(p):
    import ml_dtypes
    m = np.zeros((4, 8, 128, 512), dtype=np.float32)
    stripes = STRIPES[p]
    for j in range(4):
        s = stripes[j]
        q0 = s * 512
        for w in range(8):
            kc = KCDM[j] - 8 + w
            kk = kc * 128 + np.arange(128)[:, None]
            qq = q0 + np.arange(512)[None, :]
            m[j, w] = (kk <= qq).astype(np.float32)
    return m.astype(ml_dtypes.bfloat16)


def kernel(L0_attn_x, L0_attn_W_attn, L0_attn_b_attn, L0_attn_W_proj,
           L0_attn_b_proj, L0_attn_score_gain, L0_ffn_clip_x, L0_ffn_clip_down,
           L0_ffn_w_gate, L0_ffn_w_up, L0_ffn_w_down):
    x = np.ascontiguousarray(np.asarray(L0_attn_x, dtype=np.float32))
    Wa_ = np.ascontiguousarray(np.asarray(L0_attn_W_attn, dtype=np.float32))
    ba_ = np.ascontiguousarray(np.asarray(L0_attn_b_attn, dtype=np.float32))
    Wp_ = np.ascontiguousarray(np.asarray(L0_attn_W_proj, dtype=np.float32))
    bp_ = np.ascontiguousarray(np.asarray(L0_attn_b_proj, dtype=np.float32))
    g_ = np.asarray(L0_attn_score_gain, dtype=np.float32).reshape(1)
    cx_ = np.asarray(L0_ffn_clip_x, dtype=np.float32).reshape(1)
    cd_ = np.asarray(L0_ffn_clip_down, dtype=np.float32).reshape(1)
    wg_ = np.ascontiguousarray(np.asarray(L0_ffn_w_gate, dtype=np.float32))
    wu_ = np.ascontiguousarray(np.asarray(L0_ffn_w_up, dtype=np.float32))
    wd_ = np.ascontiguousarray(np.asarray(L0_ffn_w_down, dtype=np.float32))
    ident = np.eye(128, dtype=np.float32)

    nc, _ = build_program(debug=False)
    in_maps = []
    for c in range(8):
        b, p = c // 2, c % 2
        rows = np.concatenate([np.arange(s * 512, (s + 1) * 512) for s in STRIPES[p]])
        xb = np.ascontiguousarray(x[b])
        in_maps.append({
            "x": xb, "xq": np.ascontiguousarray(xb[rows]),
            "Wa": Wa_, "ba": ba_, "Wp": Wp_, "bp": bp_,
            "gsc": g_, "clipx": cx_, "clipd": cd_,
            "wg": wg_, "wu": wu_, "wd": wd_,
            "ident": ident, "m01": make_masks(p),
        })
    res = run_bass_kernel_spmd(nc, in_maps, core_ids=list(range(8)))
    outp = np.empty((B, T, C), dtype=np.float32)
    for c in range(8):
        b, p = c // 2, c % 2
        o = res.results[c]["out"]
        for jj, s in enumerate(STRIPES[p]):
            outp[b, s * 512:(s + 1) * 512, :] = o[jj * 512:(jj + 1) * 512, :]
    return outp
